# version 21 (final)
# baseline (speedup 1.0000x reference)
import numpy as np
from contextlib import ExitStack

import concourse.bass as bass
import concourse.mybir as mybir
from concourse.bass_utils import run_bass_kernel_spmd

F32 = mybir.dt.float32
BF16 = mybir.dt.bfloat16
AF = mybir.ActivationFunctionType
ALU = mybir.AluOpType
AX = mybir.AxisListType

RND = 1.5 * 2.0 ** 23
B, S, D, H, DK, I = 8, 2048, 1024, 16, 64, 2816
NT = S // 128
NC_ = D // 128
NIC = I // 128


class Sem:
    def __init__(self, ctx, nc, name):
        self.h = ctx.enter_context(nc.semaphore(name))
        self.v = 0

    def inc(self, k=1):
        self.v += k
        return self.v


class Prog:
    ENGS = ("sync", "gpsimd", "tensor", "vector", "scalar")

    def __init__(self):
        self.ops = {e: [] for e in self.ENGS}

    def add(self, eng, fn):
        self.ops[eng].append(fn)

    def wait(self, eng, sem, val=None):
        v = sem.v if val is None else val
        if v > 0:
            self.ops[eng].append(lambda E, s=sem.h, v=v: E.wait_ge(s, v))


def build_nc(stage=99, dumps=()):
    dumps = set(dumps)
    nc = bass.Bass("TRN2", target_bir_lowering=False, detect_race_conditions=False)

    def din(name, shape, dt=F32):
        return nc.dram_tensor(name, shape, dt, kind="ExternalInput").ap()

    x_in = din("x", [S, D])
    c_f_in = din("c_f", [128, NC_])
    wq_in = din("wq", [H, D, DK])
    wk_in = din("wk", [H, D, DK])
    wv_in = din("wv", [H, D, DK])
    bq_in = din("bq_f", [128, NC_])
    bk_in = din("bk_f", [128, NC_])
    bv_in = din("bv_f", [128, NC_])
    bo_in = din("bo_row", [1, D])
    wo_in = din("wo", [D, D])
    wmod_in = din("wmod", [D, 6 * D])
    clipx_in = din("clipx", [1, 1])
    clipd_in = din("clipd", [1, 1])
    wgn_in = din("wg_nat", [I, D])
    wun_in = din("wu_nat", [I, D])
    wdn_in = din("wd_nat", [D, I])
    wgT_in = din("wgT", [D, I])
    wuT_in = din("wuT", [D, I])
    wdT_in = din("wdT", [I, D])

    out_hbm = nc.dram_tensor("out", [S, D], F32, kind="ExternalOutput").ap()

    dump_aps = {}

    def dump(name, shape, dt=F32):
        if name in dumps:
            dump_aps[name] = nc.dram_tensor("d_" + name, shape, dt, kind="ExternalOutput").ap()
            return dump_aps[name]
        return None

    u_scr = nc.dram_tensor("u_scr", [NIC, 128, S], F32, kind="Internal").ap()
    h2q_scr = nc.dram_tensor("h2q_scr", [NIC, 128, S], BF16, kind="Internal").ap()
    cc1_in = nc.dram_tensor("cc1_in", [1, 8], F32, kind="Internal")
    cc1_out = nc.dram_tensor("cc1_out", [1, 8], F32, kind="Internal")
    cc2_in = nc.dram_tensor("cc2_in", [1, 8], F32, kind="Internal")
    cc2_out = nc.dram_tensor("cc2_out", [1, 8], F32, kind="Internal")

    ident = nc.inline_tensor(np.eye(128, dtype=np.float32), name="ident").ap()
    ones_row_c = nc.inline_tensor(np.ones((1, 128), dtype=np.float32), name="ones_row").ap()
    import ml_dtypes as _mld
    onescol_c = nc.inline_tensor(np.ones((128, 1), dtype=np.float32), name="onescol").ap()
    onescolb_c = nc.inline_tensor(np.ones((128, 1), dtype=_mld.bfloat16), name="onescolb").ap()
    mask_np = (np.arange(128)[None, :] >= np.arange(128)[:, None])
    mask_c = nc.inline_tensor(mask_np.astype(_mld.bfloat16), name="mask_c").ap()

    prog = Prog()

    with ExitStack() as ctx:
        E = ctx.enter_context

        x_sb = E(nc.sbuf_tensor("x_sb", [128, NT, D], F32))
        hfb = E(nc.sbuf_tensor("hfb", [128, NC_ * S], BF16))
        tmp = E(nc.sbuf_tensor("tmp", [128, 1024], F32))
        stats = E(nc.sbuf_tensor("stats", [128, 64], F32))
        mod_sb = E(nc.sbuf_tensor("mod_sb", [128, 112], F32))
        smalls = E(nc.sbuf_tensor("smalls", [128, 16], F32))
        hmax_acc = E(nc.sbuf_tensor("hmax_acc", [128, 32], F32))
        umax_acc = E(nc.sbuf_tensor("umax_acc", [128, 96], F32))
        swg_sb = E(nc.sbuf_tensor("swg_sb", [128, NIC], F32))
        swu_sb = E(nc.sbuf_tensor("swu_sb", [128, NIC], F32))
        swd_sb = E(nc.sbuf_tensor("swd_sb", [128, NC_], F32))
        sswg_sb = E(nc.sbuf_tensor("sswg_sb", [128, NIC], F32))
        sswu_sb = E(nc.sbuf_tensor("sswu_sb", [128, NIC], F32))
        c_sb = E(nc.sbuf_tensor("c_sb", [128, NC_], F32))
        silu_c = E(nc.sbuf_tensor("silu_c", [128, NC_], BF16))
        saca_bf = E(nc.sbuf_tensor("saca_bf", [128, NC_], BF16))
        bq_sb = E(nc.sbuf_tensor("bq_sb", [128, NC_], F32))
        bk_sb = E(nc.sbuf_tensor("bk_sb", [128, NC_], F32))
        bv_sb = E(nc.sbuf_tensor("bv_sb", [128, NC_], F32))
        bqp = E(nc.sbuf_tensor("bqp", [128, 4], F32))
        bvp_sb = E(nc.sbuf_tensor("bvp_sb", [128, NC_], F32))
        bvp_bf = E(nc.sbuf_tensor("bvp_bf", [128, NC_], BF16))
        negc_sb = E(nc.sbuf_tensor("negc_sb", [128, 2], F32))
        csr = E(nc.sbuf_tensor("csr", [1, 16], F32))
        rowA = E(nc.sbuf_tensor("rowA", [1, D], F32))
        rowB = E(nc.sbuf_tensor("rowB", [1, 1024], F32))
        rowD = E(nc.sbuf_tensor("rowD", [1, 128], F32))
        id_sb = E(nc.sbuf_tensor("id_sb", [128, 128], F32))
        ones_sb = E(nc.sbuf_tensor("ones_sb", [1, 128], F32))
        onescol_sb = E(nc.sbuf_tensor("onescol_sb", [128, 1], F32))
        onescol_bf = E(nc.sbuf_tensor("onescol_bf", [128, 1], BF16))
        mask_sb = E(nc.sbuf_tensor("mask_sb", [128, 128], BF16))

        qbase = nc.sbuf_base
        qtop = nc.sbuf_top

        def place(name, shape, dt, off):
            t = nc.alloc_sbuf_tensor_at(name, shape, dt, offset=qbase + off)
            nbytes = int(np.prod(shape[1:])) * mybir.dt.size(dt)
            assert qbase + off + nbytes <= qtop, (name, off, nbytes, qtop - qbase)
            return t

        KB = 1024
        htb = place("htb", [128, NT * D], BF16, 0)
        wmod_bf = place("wmod_bf", [128, NC_, 512], BF16, 33 * KB)
        wmod_bf2 = place("wmod_bf2", [128, NC_, 512], BF16, 41 * KB)
        wq_bf = place("wq_bf", [128, 2, NC_, 128], BF16, 33 * KB)
        wk_bf = place("wk_bf", [128, 2, NC_, 128], BF16, 37 * KB)
        wv_bf = place("wv_bf", [128, 2, NC_, 128], BF16, 41 * KB)
        qf = place("qf", [128, 2, S], BF16, 45 * KB)
        kf = place("kf", [128, 2, S], BF16, 53 * KB)
        wo_bf = place("wo_bf", [128, NC_, D], BF16, 76 * KB + 512)
        vt = place("vt", [128, 2, NT, 130], BF16, 61 * KB)
        cb1s_bc = place("cb1s_bc", [128, D], F32, 61 * KB)
        sbs_bc = place("sbs_bc", [128, D], F32, 65 * KB)
        sq_sb = place("sq_sb", [128, 512], BF16, 69 * KB + 256)
        sq2_sb = place("sq2_sb", [128, 512], BF16, 92 * KB + 512)
        p_sb = place("p_sb", [128, 2, 1024], BF16, 70 * KB + 512)
        ctx_stage = place("ctx_stage", [128, 1024], BF16, 74 * KB + 512)
        natw = place("natw", [128, 4, 1024], F32, 76 * KB + 512)
        xqtile = place("xqtile", [128, 2, 1024], BF16, 70 * KB + 512)
        h2scr = place("h2scr", [128, 1024], F32, 70 * KB + 512)
        d1stg = place("d1stg", [128, 2, 1024], F32, 80 * KB + 512)
        wd_pre = place("wd_pre", [128, NIC, D], BF16, 0)
        wgT_s = place("wgT_s", [128, 2, NC_, 128], F32, 44 * KB)
        wuT_s = place("wuT_s", [128, 2, NC_, 128], F32, 52 * KB)
        wgq = place("wgq", [128, 2, NC_, 128], BF16, 60 * KB)
        wuq = place("wuq", [128, 2, NC_, 128], BF16, 64 * KB)
        sg_sb = place("sg_sb", [128, 2, 512], F32, 68 * KB)
        up_sb = place("up_sb", [128, 2, 512], F32, 72 * KB)
        u_sb = place("u_sb", [128, 2, 512], F32, 76 * KB)
        uin = place("uin", [128, 2, NIC, 128], F32, 44 * KB)
        h2q_s2 = place("h2q_s2", [128, 2, NIC, 128], BF16, 67 * KB)
        assert qbase + 81 * KB <= qtop, (qbase, qtop)

        PS0 = E(nc.psum_tensor("PS0", [128, 1024], F32))
        PS1 = E(nc.psum_tensor("PS1", [128, 1024], F32))
        PSC = E(nc.psum_tensor("PSC", [128, 1024], F32))
        PSB = E(nc.psum_tensor("PSB", [128, 1024], F32))

        sm = lambda name: Sem(ctx, nc, name)
        s_xld = sm("xld")
        s_cld = sm("cld")
        s_wmod = sm("wmodld")
        s_mod = sm("mod")
        s_modv = sm("modv")
        s_stat = sm("stat")
        s_rstd = sm("rstd")
        s_h = sm("h")
        s_ht = sm("ht")
        s_silu = sm("silu")
        s_fin = sm("fin")
        vch = sm("vch")
        sch = sm("sch")

        def vop(fn, dep=False):
            if dep:
                prog.wait("vector", vch, vch.v)
            prog.add("vector", lambda E_, f=fn: f(E_).then_inc(vch.h, 1))
            return vch.inc()

        def vdep(fn):
            return vop(fn, dep=True)

        def sop(fn, dep=False):
            if dep:
                prog.wait("scalar", sch, sch.v)
            prog.add("scalar", lambda E_, f=fn: f(E_).then_inc(sch.h, 1))
            return sch.inc()

        def sdep(fn):
            return sop(fn, dep=True)

        xv = x_in.rearrange("(t p) d -> p t d", p=128)
        for q in range(4):
            prog.add("sync", lambda E_, q=q: E_.dma_start(
                out=x_sb[:, 4 * q:4 * q + 4, :], in_=xv[:, 4 * q:4 * q + 4, :]).then_inc(s_xld.h, 16))
            s_xld.inc(16)
        for ap_out, ap_in in ((id_sb[:], ident), (ones_sb[:], ones_row_c),
                              (onescol_sb[:], onescol_c), (c_sb[:], c_f_in),
                              (bq_sb[:], bq_in), (bk_sb[:], bk_in), (bv_sb[:], bv_in),
                              (rowA[:], bo_in), (csr[0:1, 8:9], clipx_in),
                              (csr[0:1, 9:10], clipd_in)):
            prog.add("sync", lambda E_, o=ap_out, i=ap_in: E_.dma_start(out=o, in_=i).then_inc(s_cld.h, 16))
            s_cld.inc(16)

        prog.add("sync", lambda E_: E_.dma_start(out=mask_sb[:], in_=mask_c).then_inc(s_cld.h, 16))
        s_cld.inc(16)
        prog.add("sync", lambda E_: E_.dma_start(out=onescol_bf[:], in_=onescolb_c).then_inc(s_cld.h, 16))
        s_cld.inc(16)
        wmv = wmod_in.rearrange("(c p) m -> p c m", p=128)
        wmod_ld_vals = []
        for ch in range(12):
            buf = wmod_bf if ch % 2 == 0 else wmod_bf2
            if ch >= 2:
                prog.wait("gpsimd", s_mod, ch - 1)
            prog.add("gpsimd", lambda E_, b=buf, ch=ch: E_.dma_start(
                out=b[:], in_=wmv[:, :, 512 * ch:512 * (ch + 1)]).then_inc(s_wmod.h, 16))
            wmod_ld_vals.append(s_wmod.inc(16))

        v_k = vop(lambda E_: E_.memset(smalls[:, 4:5], 1e-5))

        prog.wait("scalar", s_cld, s_cld.v)
        prog.add("scalar", lambda E_: E_.activation(silu_c[:], c_sb[:], AF.Silu).then_inc(s_silu.h, 1))
        s_silu.inc()

        prog.wait("tensor", s_silu, 1)
        for ch in range(12):
            buf = wmod_bf if ch % 2 == 0 else wmod_bf2
            prog.wait("tensor", s_wmod, wmod_ld_vals[ch])
            for j4 in range(4):
                j = ch * 4 + j4
                for c in range(NC_):
                    first, last = (c == 0), (c == NC_ - 1)
                    def mk(b=buf, j4=j4, c=c, j=j, first=first, last=last):
                        def f(E_):
                            return E_.matmul(PSB[:, j:j + 1], b[:, c, 128 * j4:128 * (j4 + 1)],
                                             silu_c[:, c:c + 1], start=first, stop=last)
                        return f
                    fn = mk()
                    if last:
                        prog.add("tensor", lambda E_, f=fn: f(E_).then_inc(s_mod.h, 1))
                    else:
                        prog.add("tensor", fn)
            s_mod.inc(4)

        prog.wait("vector", s_mod, 48)
        vop(lambda E_: E_.tensor_copy(mod_sb[:, 0:48], PSB[:, 0:48]))
        vdep(lambda E_: E_.tensor_scalar_add(mod_sb[:, 48:56], mod_sb[:, 8:16], 1.0))
        vdep(lambda E_: E_.tensor_scalar_add(mod_sb[:, 72:80], mod_sb[:, 32:40], 1.0))
        vdep(lambda E_: E_.reciprocal(mod_sb[:, 88:96], mod_sb[:, 48:56]))
        vdep(lambda E_: E_.tensor_mul(mod_sb[:, 56:64], mod_sb[:, 0:8], mod_sb[:, 88:96]))
        for (si, dst) in ((16, 64), (40, 80)):
            vdep(lambda E_, s=si: E_.tensor_scalar_mul(mod_sb[:, 96:104], mod_sb[:, s:s + 8], -1.0))
            vdep(lambda E_, s=si: E_.tensor_max(mod_sb[:, 104:112], mod_sb[:, s:s + 8], mod_sb[:, 96:104]))
            vdep(lambda E_: E_.tensor_scalar_max(mod_sb[:, 96:104], mod_sb[:, 104:112], 0.25))
            vdep(lambda E_, s=si: E_.tensor_scalar(
                mod_sb[:, 104:112], mod_sb[:, s:s + 8], 0.0, 2.0, ALU.is_ge, ALU.mult))
            vdep(lambda E_, d=dst: E_.scalar_tensor_tensor(
                mod_sb[:, d:d + 8], mod_sb[:, 104:112], -1.0, mod_sb[:, 96:104], ALU.add, ALU.mult))
        v_modv = vdep(lambda E_: E_.tensor_copy(saca_bf[:], mod_sb[:, 56:64]))

        ht = htb[:].rearrange("p (t d) -> p t d", t=NT)
        for i in range(NT):
            prog.wait("vector", s_xld, 16 * (i // 4 + 1))
            vop(lambda E_, i=i: E_.bn_stats(tmp[:, 512 + 16 * i:518 + 16 * i], x_sb[:, i, 0:512]))
            vop(lambda E_, i=i: E_.bn_stats(tmp[:, 520 + 16 * i:526 + 16 * i], x_sb[:, i, 512:1024]))
        bnall = tmp[:, 512:768].rearrange("p (t u k) -> p t u k", u=2, k=8)
        means4 = tmp[:, 768:800].rearrange("p (t u) -> p t u", u=2)
        m2s4 = tmp[:, 800:832].rearrange("p (t u) -> p t u", u=2)
        vdep(lambda E_: E_.tensor_tensor(means4, bnall[:, :, :, 1], bnall[:, :, :, 4], ALU.add))
        vdep(lambda E_: E_.tensor_tensor(m2s4, bnall[:, :, :, 2], bnall[:, :, :, 5], ALU.add))
        vdep(lambda E_: E_.tensor_tensor(
            tmp[:, 832:864].rearrange("p (t u) -> p t u", u=2),
            bnall[:, :, :, 1], bnall[:, :, :, 1], ALU.mult))
        vdep(lambda E_: E_.tensor_tensor(
            tmp[:, 864:896].rearrange("p (t u) -> p t u", u=2),
            bnall[:, :, :, 4], bnall[:, :, :, 4], ALU.mult))
        vdep(lambda E_: E_.tensor_add(
            tmp[:, 832:864].rearrange("p (t u) -> p t u", u=2),
            tmp[:, 832:864].rearrange("p (t u) -> p t u", u=2),
            tmp[:, 864:896].rearrange("p (t u) -> p t u", u=2)))
        vdep(lambda E_: E_.reduce_sum(tmp[:, 704:720], means4, AX.X))
        vdep(lambda E_: E_.reduce_sum(tmp[:, 720:736], m2s4, AX.X))
        vdep(lambda E_: E_.reduce_sum(tmp[:, 736:752], tmp[:, 832:864].rearrange("p (t u) -> p t u", u=2), AX.X))
        vdep(lambda E_: E_.tensor_scalar_mul(stats[:, 0:16], tmp[:, 704:720], 0.25))
        vdep(lambda E_: E_.tensor_mul(tmp[:, 752:768], stats[:, 0:16], stats[:, 0:16]))
        vdep(lambda E_: E_.tensor_scalar_mul(tmp[:, 832:848], tmp[:, 736:752], 0.25))
        vdep(lambda E_: E_.scalar_tensor_tensor(
            tmp[:, 848:864], tmp[:, 720:736], 1.0 / 1024.0, tmp[:, 832:848], ALU.mult, ALU.add))
        v_stat = vdep(lambda E_: E_.tensor_sub(tmp[:, 0:16], tmp[:, 848:864], tmp[:, 752:768]))
        prog.wait("scalar", vch, v_stat)
        s_sqrt = sop(lambda E_: E_.activation(tmp[:, 16:32], tmp[:, 0:16], AF.Sqrt, bias=smalls[:, 4:5]))
        prog.wait("vector", sch, s_sqrt)
        vop(lambda E_: E_.reciprocal(stats[:, 16:32], tmp[:, 16:32]))

        v_h = []
        first = True
        for i in range(NT):
            f = lambda E_, i=i: E_.tensor_scalar(
                ht[:, i, :], x_sb[:, i, :], stats[:, i:i + 1], stats[:, 16 + i:17 + i],
                ALU.subtract, ALU.mult)
            v_h.append(vop(f, dep=first))
            first = False

        hf = hfb[:].rearrange("p (c s) -> p c s", c=NC_)
        for i in range(NT):
            prog.wait("scalar", vch, v_h[i])
            prog.add("scalar", lambda E_, i=i: E_.dma_start_transpose(
                out=hf[:, :, 128 * i:128 * (i + 1)],
                in_=ht[:, i, :].rearrange("p (c f) -> p c f", c=NC_)).then_inc(s_ht.h, 16))
            s_ht.inc(16)

        if "h_f" in dumps:
            ap = dump("h_f", [128, NC_ * S], BF16)
            prog.wait("gpsimd", s_ht, s_ht.v)
            prog.add("gpsimd", lambda E_, ap=ap: E_.dma_start(out=ap, in_=hfb[:]).then_inc(s_fin.h, 16))
            s_fin.inc(16)
        if "mod" in dumps:
            ap = dump("mod", [128, 112], F32)
            prog.wait("gpsimd", vch, v_modv)
            prog.add("gpsimd", lambda E_, ap=ap: E_.dma_start(out=ap, in_=mod_sb[:]).then_inc(s_fin.h, 16))
            s_fin.inc(16)
        if "stats" in dumps:
            ap = dump("stats", [128, 64], F32)
            prog.wait("gpsimd", vch, vch.v)
            prog.add("gpsimd", lambda E_, ap=ap: E_.dma_start(out=ap, in_=stats[:]).then_inc(s_fin.h, 16))
            s_fin.inc(16)

        if stage <= 1:
            prog.wait("sync", s_ht, s_ht.v)
            prog.wait("sync", vch, v_modv)
            prog.add("sync", lambda E_: E_.dma_start(
                out=out_hbm.rearrange("(t p) d -> p t d", p=128), in_=x_sb[:]).then_inc(s_fin.h, 16))
            s_fin.inc(16)
            prog.wait("sync", s_fin, s_fin.v)
            prog.wait("sync", s_cld, s_cld.v)
            emit(nc, prog)
            return nc

        s_wld = sm("wld")
        s_t = sm("tmm")
        s_move = sm("cmove")

        ctxv = htb[:].rearrange("p (g s) -> p g s", g=NC_)
        HT_ALL = s_ht.v

        vt5 = vt[:].rearrange("p b t (h k) -> p b t h k", k=65)
        v_vones = vop(lambda E_: E_.memset(vt5[:, :, :, :, 64:65], 1.0))
        zrow = E(nc.sbuf_tensor("zrow", [1, 65], F32))
        v_zrow = vop(lambda E_: E_.memset(zrow[:], 0.0))

        npairs = 2 if stage == 2 else 8
        st_exp_last = [0, 0]
        st_ctx_free = 0
        sv_ctx_free = 0
        st_avlast = {}
        sv_vev_last = {}
        ss_qkev_last = {}
        sv_bq = {}
        q_evac_sch = {}
        exp_sch_prev_pair = 0
        vch_prev_pair = 0
        prev_iter_exp = 0
        prev_iter_avlast = 0
        psc_free = [0]

        wprep_done = {}
        wuse_done = {}
        negc_pend = [0]

        def load_pair(g):
            p = g % 2
            if g < 2:
                prog.wait("gpsimd", s_mod, 48)
            if g >= 2:
                prog.wait("gpsimd", s_t, wuse_done.get(g - 2, 0))
            for (wdst, wsrc) in ((wq_bf, wq_in), (wk_bf, wk_in), (wv_bf, wv_in)):
                for h2 in range(2):
                    prog.add("gpsimd", lambda E_, o=wdst, s=wsrc, p=p, g=g, h2=h2: E_.dma_start(
                        out=o[:, p, :, 64 * h2:64 * h2 + 64],
                        in_=s[2 * g + h2].rearrange("(c p) k -> p c k", p=128)).then_inc(s_wld.h, 16))
                    s_wld.inc(16)
            wld_done = s_wld.v
            prog.wait("vector", s_wld, wld_done)
            first = (g == 0)
            for wdst in (wq_bf, wk_bf, wv_bf):
                for c in range(NC_):
                    f = lambda E_, o=wdst, p=p, c=c: E_.tensor_scalar_mul(
                        o[:, p, c, :], o[:, p, c, :], mod_sb[:, 48 + c:49 + c])
                    vop(f, dep=first)
                    first = False
            wprep_done[g] = vch.v

        def qkv_pair(g):
            p = g % 2
            if g >= 2:
                prog.wait("tensor", s_t, st_avlast[g - 2])
            prog.wait("tensor", vch, wprep_done[g])
            if g >= 1:
                prog.wait("tensor", sch, exp_sch_prev_pair)
                prog.wait("tensor", vch, vch_prev_pair)
            for bi, wdst in enumerate((wq_bf, wk_bf, wv_bf)):
                for c in range(NC_):
                    fn = lambda E_, o=wdst, p=p, c=c, bi=bi: E_.matmul(
                        PS1[:, 768 + bi:769 + bi], o[:, p, c, :], saca_bf[:, c:c + 1],
                        start=(c == 0), stop=(c == NC_ - 1))
                    if c == NC_ - 1:
                        prog.add("tensor", lambda E_, f=fn: f(E_).then_inc(s_t.h, 1))
                        s_t.inc()
                    else:
                        prog.add("tensor", fn)
            t_bias = s_t.v
            prog.wait("vector", s_t, t_bias)
            vdep(lambda E_, p=p, g=g: E_.tensor_add(bqp[:, 2 * p:2 * p + 1], PS1[:, 768:769], bq_sb[:, g:g + 1]))
            vdep(lambda E_, p=p, g=g: E_.tensor_add(bqp[:, 2 * p + 1:2 * p + 2], PS1[:, 769:770], bk_sb[:, g:g + 1]))
            v_bq = vch.v
            vdep(lambda E_, g=g: E_.tensor_add(bvp_sb[:, g:g + 1], PS1[:, 770:771], bv_sb[:, g:g + 1]))
            v_bv = vch.v
            prog.wait("scalar", vch, v_bq)
            t_q, t_k = [], []
            pend_norm = [None]
            norm_mm_vals = {}
            v_qred = [0]

            def flush_norm():
                if pend_norm[0] is None:
                    return
                sv, j, buf = pend_norm[0]
                pend_norm[0] = None
                prog.wait("tensor", sch, sv)
                sl = j % 4
                if j == 4:
                    prog.wait("tensor", vch, v_qred[0])
                fn = lambda E_, sl=sl, buf=buf: E_.matmul(
                    PSB[0:1, 512 * (sl % 2):512 * (sl % 2 + 1)], onescol_bf[:],
                    buf[:], start=(sl < 2), stop=(sl >= 2))
                prog.add("tensor", lambda E_, f=fn: f(E_).then_inc(s_t.h, 1))
                s_t.inc()
                norm_mm_vals[j] = s_t.v

            for tc in range(4):
                ps = (PS0, PSC)[tc % 2]
                prog.wait("tensor", s_ht, 16 * 4 * (tc + 1))
                if tc >= 2:
                    prog.wait("tensor", sch, q_evac_sch[(g, tc - 2)])
                flush_norm()
                for (wdst, cols, tl) in ((wq_bf, (0, 512), t_q), (wk_bf, (512, 1024), t_k)):
                    for c in range(NC_):
                        fn = lambda E_, o=wdst, p=p, c=c, tc=tc, ps=ps, cols=cols: E_.matmul(
                            ps[:, cols[0]:cols[1]], o[:, p, c, :],
                            hf[:, c, 512 * tc:512 * (tc + 1)],
                            start=(c == 0), stop=(c == NC_ - 1))
                        if c == NC_ - 1:
                            prog.add("tensor", lambda E_, f=fn: f(E_).then_inc(s_t.h, 1))
                            s_t.inc()
                        else:
                            prog.add("tensor", fn)
                    tl.append(s_t.v)
                prog.wait("scalar", s_t, t_k[tc])
                sop(lambda E_, p=p, tc=tc, ps=ps: E_.activation(
                    qf[:, p, 512 * tc:512 * (tc + 1)], ps[:, 0:512], AF.Identity,
                    bias=bqp[:, 2 * p:2 * p + 1]))
                sop(lambda E_, p=p, tc=tc, ps=ps: E_.activation(
                    kf[:, p, 512 * tc:512 * (tc + 1)], ps[:, 512:1024], AF.Identity,
                    bias=bqp[:, 2 * p + 1:2 * p + 2]))
                q_evac_sch[(g, tc)] = sch.v
            t_v = []
            vt_base = len(vt_ev)
            for i in range(NT):
                vps = PS1[:, 0:128] if i % 2 == 0 else PS1[:, 512:640]
                prog.wait("tensor", s_ht, 16 * (i + 1))
                if i % 2 == 1 and i == 1:
                    prog.wait("tensor", vch, v_bv)
                if i >= 2:
                    prog.wait("tensor", vch, vt_ev[vt_base + i - 2])
                if i == 6:
                    prog.wait("vector", s_t, norm_mm_vals[3])
                    vdep(lambda E_: E_.tensor_reduce(
                        csr[0:1, 0:1], PSB[0:1, 0:1024], AX.X, ALU.max))
                    v_qred[0] = vch.v
                if 2 <= i <= 9:
                    flush_norm()
                if 1 <= i <= 8:
                    j = i - 1
                    buf = sq_sb if j % 2 == 0 else sq2_sb
                    src = qf if j < 4 else kf
                    if j - 2 in norm_mm_vals:
                        prog.wait("scalar", s_t, norm_mm_vals[j - 2])
                    sdep(lambda E_, src=src, p=p, sl=j % 4, buf=buf: E_.activation(
                        buf[:], src[:, p, 512 * sl:512 * (sl + 1)], AF.Square))
                    pend_norm[0] = (sch.v, j, buf)
                if i == 10:
                    prog.wait("vector", s_t, norm_mm_vals[7])
                    vdep(lambda E_: E_.tensor_reduce(
                        csr[0:1, 1:2], PSB[0:1, 0:1024], AX.X, ALU.max))
                    vdep(lambda E_: E_.tensor_add(csr[0:1, 2:3], csr[0:1, 0:1], csr[0:1, 1:2]))
                    vdep(lambda E_: E_.tensor_scalar_mul(csr[0:1, 3:4], csr[0:1, 2:3], -0.0625))
                    negc_pend[0] = vch.v
                for c in range(NC_):
                    fn = lambda E_, p=p, c=c, i=i, vps=vps: E_.matmul(
                        vps, hf[:, c, 128 * i:128 * (i + 1)], wv_bf[:, p, c, :],
                        start=(c == 0), stop=(c == NC_ - 1))
                    if c == NC_ - 1:
                        prog.add("tensor", lambda E_, f=fn: f(E_).then_inc(s_t.h, 1))
                        s_t.inc()
                    else:
                        prog.add("tensor", fn)
                t_v.append(s_t.v)
                prog.wait("vector", s_t, t_v[i])
                fn = lambda E_, p=p, i=i, vps=vps: E_.tensor_copy(
                    vt5[:, p, i, :, 0:64],
                    vps.rearrange("p (h k) -> p h k", k=64))
                vdep(fn)
                vt_ev.append(vch.v)
            sv_vev_last[g] = vch.v
            wuse_done[g] = s_t.v

            ss_qkev_last[g] = sch.v

            return ss_qkev_last[g]

        def attn_pair(g, s_negc):
            nonlocal exp_sch_prev_pair, vch_prev_pair, sv_ctx_free, st_ctx_free, prev_iter_exp, prev_iter_avlast
            p = g % 2
            prog.wait("tensor", sch, s_negc)
            prog.wait("tensor", vch, sv_vev_last[g])
            prog.wait("tensor", vch, negc_pend[0])
            fn = lambda E_: E_.matmul(PSB[:, 0:1], ones_sb[:], csr[0:1, 3:4],
                                      start=True, stop=True)
            prog.add("tensor", lambda E_, f=fn: f(E_).then_inc(s_t.h, 1))
            s_t.inc()
            prog.wait("scalar", s_t, s_t.v)
            sop(lambda E_: E_.activation(negc_sb[:, 0:1], PSB[:, 0:1], AF.Copy))
            exp_vals = {}
            for h2 in range(2):
                for qc in range(2):
                    q0 = 1024 * qc
                    jmax = 8 * qc + 8
                    t_av_prev2 = {}
                    prog.wait("tensor", sch, prev_iter_exp)
                    all_pieces = {}
                    v_masks = {}

                    def emit_av(j, h2=h2, qc=qc, jmax=jmax):
                        if j == 0:
                            prog.wait("tensor", sch, psc_free[0])
                        prog.wait("tensor", sch, exp_vals[j])
                        if v_masks.get(j):
                            prog.wait("tensor", vch, v_masks[j])
                        for (a, b_, stp) in all_pieces[j]:
                            fn = lambda E_, p=p, h2=h2, j=j, a=a, b_=b_, stp=stp: E_.matmul(
                                PSC[0:65, a:b_], vt5[:, p, j, h2, :],
                                p_sb[:, j % 2, a:b_], start=(j == 0), stop=stp)
                            last = (a, b_, stp) == all_pieces[j][-1]
                            if last:
                                prog.add("tensor", lambda E_, f=fn: f(E_).then_inc(s_t.h, 1))
                                s_t.inc()
                            else:
                                prog.add("tensor", fn)
                        t_av_prev2[j] = s_t.v

                    for j in range(jmax):
                        c0 = max(0, 128 * j - q0)
                        c1 = max(0, 128 * (j + 1) - q0)
                        sps = (PS0, PS1)[j % 2]
                        pieces = []
                        c = c0
                        if j < jmax - 1 and c1 > c0:
                            pieces.append((c0, c1, True))
                            c = c1
                        while c < 1024:
                            ce = min(1024, (c // 512 + 1) * 512)
                            pieces.append((c, ce, j == jmax - 1))
                            c = ce
                        all_pieces[j] = pieces
                        if j >= 2:
                            prog.wait("tensor", sch, exp_vals[j - 2])
                        for (a, b_, _stp) in pieces:
                            fn = lambda E_, p=p, h2=h2, j=j, a=a, b_=b_, sps=sps, q0=q0: E_.matmul(
                                sps[:, a:b_], kf[64 * h2:64 * h2 + 64, p, 128 * j:128 * (j + 1)],
                                qf[64 * h2:64 * h2 + 64, p, q0 + a:q0 + b_],
                                start=True, stop=True)
                            last = (a, b_, _stp) == pieces[-1]
                            if last:
                                prog.add("tensor", lambda E_, f=fn: f(E_).then_inc(s_t.h, 1))
                                s_t.inc()
                            else:
                                prog.add("tensor", fn)
                        t_s = s_t.v
                        prog.wait("scalar", s_t, t_s)
                        if j >= 2:
                            prog.wait("scalar", s_t, t_av_prev2[j - 2])
                        else:
                            prog.wait("scalar", s_t, prev_iter_avlast)
                        sop(lambda E_, j=j, c0=c0, sps=sps: E_.activation(
                            p_sb[:, j % 2, c0:1024], sps[:, c0:1024], AF.Exp,
                            scale=0.125, bias=negc_sb[:, 0:1]))
                        exp_vals[j] = sch.v
                        diag = (128 * j >= q0)
                        if diag:
                            prog.wait("vector", sch, exp_vals[j])
                            vop(lambda E_, j=j, c0=c0: E_.tensor_mul(
                                p_sb[:, j % 2, c0:c0 + 128], p_sb[:, j % 2, c0:c0 + 128], mask_sb[:]))
                            v_masks[j] = vch.v
                        if j >= 1:
                            emit_av(j - 1)
                    emit_av(jmax - 1)
                    t_av_last = s_t.v
                    st_avlast[g] = t_av_last
                    prev_iter_avlast = t_av_last
                    prog.wait("scalar", s_t, t_av_last)
                    sop(lambda E_: E_.activation(rowB[:], PSC[64:65, 0:1024], AF.Ln))
                    sdep(lambda E_: E_.activation(rowB[:], rowB[:], AF.Exp, scale=-1.0))
                    s_r = sch.v
                    prog.wait("scalar", vch, sv_ctx_free)
                    sop(lambda E_: E_.activation(ctx_stage[0:64, :], PSC[0:64, 0:1024], AF.Copy))
                    s_ctxraw = sch.v
                    prog.wait("tensor", sch, s_r)
                    prog.wait("tensor", vch, sv_ctx_free)
                    for half in range(2):
                        fn = lambda E_, half=half: E_.matmul(
                            PSB[0:64, 512 * half:512 * (half + 1)], ones_sb[0:1, 0:64],
                            rowB[0:1, 512 * half:512 * (half + 1)], start=True, stop=True)
                        if half == 1:
                            prog.add("tensor", lambda E_, f=fn: f(E_).then_inc(s_t.h, 1))
                            s_t.inc()
                        else:
                            prog.add("tensor", fn)
                    t_bc = s_t.v
                    prog.wait("vector", sch, s_ctxraw)
                    prog.wait("vector", s_t, t_bc)
                    if h2 == 0:
                        vop(lambda E_, g=g, q0=q0: E_.tensor_mul(
                            ctxv[0:64, g, q0:q0 + 1024], ctx_stage[0:64, :], PSB[0:64, 0:1024]))
                        sv_ctx_free = vch.v
                        st_ctx_free = sch.v
                    else:
                        vop(lambda E_: E_.tensor_mul(
                            ctx_stage[0:64, :], ctx_stage[0:64, :], PSB[0:64, 0:1024]))
                        sv_ctx_free = vch.v
                        v_norm = vch.v
                        prog.wait("gpsimd", s_ht, HT_ALL)
                        prog.wait("gpsimd", vch, v_norm)
                        prog.add("gpsimd", lambda E_, g=g, q0=q0: E_.dma_start(
                            out=ctxv[64:128, g, q0:q0 + 1024], in_=ctx_stage[0:64, :]).then_inc(s_move.h, 16))
                        s_move.inc(16)
                    psc_free[0] = s_ctxraw
                    if h2 == 1:
                        prog.wait("scalar", s_move, s_move.v)
                    prev_iter_exp = sch.v
                    emit_nat(3)
                    if g >= 6:
                        emit_wo_dma(1)
            exp_sch_prev_pair = sch.v
            vch_prev_pair = vch.v

        s_nat = sm("natld")
        nat_jobs = []
        for ic_ in range(NIC):
            nat_jobs.append((wgn_in.rearrange("(a p) h -> p a h", p=128)[:, ic_, :], swg_sb[:, ic_:ic_ + 1], 1024, None))
        for ic_ in range(NIC):
            nat_jobs.append((wun_in.rearrange("(a p) h -> p a h", p=128)[:, ic_, :], swu_sb[:, ic_:ic_ + 1], 1024, None))
        wdr_ = wdn_in.rearrange("(a p) i -> p a i", p=128)
        for hc_ in range(NC_):
            for pc_, (c0_, c1_) in enumerate(((0, 1024), (1024, 2048), (2048, 2816))):
                nat_jobs.append((wdr_[:, hc_, c0_:c1_], umax_acc[:, 89 + pc_:90 + pc_], c1_ - c0_,
                                 hc_ if pc_ == 2 else None))
        nat_state = {"nd": 0, "nr": 0}
        nat_red = {}
        nat_dma = {}

        def emit_nat_dma(k):
            for _ in range(k):
                n = nat_state["nd"]
                if n >= len(nat_jobs) or n - nat_state["nr"] >= 4:
                    return
                src_ap, _, w, _ = nat_jobs[n]
                if n >= 4:
                    prog.wait("sync", vch, nat_red[n - 4])
                prog.add("sync", lambda E_, s=src_ap, w=w, b=n % 4: E_.dma_start(
                    out=natw[:, b, 0:w], in_=s).then_inc(s_nat.h, 16))
                s_nat.inc(16)
                nat_dma[n] = s_nat.v
                nat_state["nd"] = n + 1

        def emit_nat(k):
            for _ in range(k):
                n = nat_state["nr"]
                if n >= len(nat_jobs):
                    return
                if n >= nat_state["nd"]:
                    emit_nat_dma(1)
                _, dst_ap, w, comb = nat_jobs[n]
                prog.wait("vector", s_nat, nat_dma[n])
                vdep(lambda E_, d=dst_ap, w=w, b=n % 4: E_.tensor_reduce(
                    d, natw[:, b, 0:w], AX.X, ALU.max, apply_absolute_value=True))
                nat_red[n] = vch.v
                if comb is not None:
                    vdep(lambda E_: E_.tensor_max(
                        umax_acc[:, 92:93], umax_acc[:, 89:90], umax_acc[:, 90:91]))
                    vdep(lambda E_, hc=comb: E_.tensor_max(
                        swd_sb[:, hc:hc + 1], umax_acc[:, 92:93], umax_acc[:, 91:92]))
                    nat_red[n] = vch.v
                nat_state["nr"] = n + 1
                emit_nat_dma(2)

        prog.wait("tensor", vch, vch.v)
        prog.wait("tensor", s_cld, s_cld.v)
        for c in range(NC_):
            fn = lambda E_, c=c: E_.transpose(
                PS0[0:1, 128 * c:128 * (c + 1)], mod_sb[:, 64 + c:65 + c], id_sb[:])
            if c == NC_ - 1:
                prog.add("tensor", lambda E_, f=fn: f(E_).then_inc(s_t.h, 1))
                s_t.inc()
            else:
                prog.add("tensor", fn)
        prog.wait("scalar", s_t, s_t.v)
        sop(lambda E_: E_.activation(rowB[:], PS0[0:1, 0:1024], AF.Copy))
        s_garow = sch.v
        prog.wait("tensor", sch, s_garow)
        for half in range(2):
            fn = lambda E_, half=half: E_.matmul(
                PSB[:, 512 * half:512 * (half + 1)], ones_sb[:],
                rowB[0:1, 512 * half:512 * (half + 1)], start=True, stop=True)
            if half == 1:
                prog.add("tensor", lambda E_, f=fn: f(E_).then_inc(s_t.h, 1))
                s_t.inc()
            else:
                prog.add("tensor", fn)
        prog.wait("scalar", s_t, s_t.v)
        prog.wait("scalar", vch, vch.v)
        sop(lambda E_: E_.activation(tmp[:, 0:1024], PSB[:, 0:1024], AF.Copy))
        s_gabc = sch.v

        wo_state = {"c": 0, "dma": 0}

        def emit_wo_dma(k):
            for _ in range(k):
                c = wo_state["c"]
                if c >= NC_:
                    return
                if c == 0:
                    assert nat_state["nr"] >= len(nat_jobs), nat_state
                    prog.wait("gpsimd", vch, nat_red[len(nat_jobs) - 1])
                prog.add("gpsimd", lambda E_, c=c: E_.dma_start(
                    out=wo_bf[:, c, :],
                    in_=wo_in.rearrange("(c p) o -> p c o", p=128)[:, c, :]).then_inc(s_wld.h, 16))
                s_wld.inc(16)
                wo_state["dma"] = s_wld.v
                wo_state["c"] = c + 1

        vt_ev = []
        load_pair(0)
        emit_nat_dma(4)
        for g in range(npairs):
            if g + 1 < npairs:
                load_pair(g + 1)
            s_negc = qkv_pair(g)
            if stage >= 3:
                attn_pair(g, s_negc)
                emit_nat(1)

        if stage == 2:
            for nm, buf, shp, w in (("qf", qf, [128, 2 * S], s_t.v),
                                    ("kf", kf, [128, 2 * S], s_t.v),
                                    ("vt", vt, [128, 2 * NT * 130], s_t.v)):
                if nm in dumps:
                    ap = dump(nm, shp, BF16)
                    prog.wait("gpsimd", sch, sch.v)
                    prog.wait("gpsimd", vch, vch.v)
                    prog.add("gpsimd", lambda E_, ap=ap, b=buf: E_.dma_start(out=ap, in_=b[:]).then_inc(s_fin.h, 16))
                    s_fin.inc(16)
            prog.wait("sync", s_fin, s_fin.v)
            prog.wait("sync", sch, sch.v)
            prog.wait("sync", vch, vch.v)
            prog.add("sync", lambda E_: E_.dma_start(
                out=out_hbm.rearrange("(t p) d -> p t d", p=128), in_=x_sb[:]).then_inc(s_fin.h, 16))
            s_fin.inc(16)
            prog.wait("sync", s_fin, s_fin.v)
            prog.wait("sync", s_cld, s_cld.v)
            emit(nc, prog)
            return nc

        emit_wo_dma(NC_)
        prog.wait("vector", vch, vch.v)
        vop(lambda E_: E_.tensor_copy(bvp_bf[:], bvp_sb[:]))
        v_bvbf = vch.v
        prog.wait("tensor", s_wld, wo_state["dma"])
        prog.wait("tensor", vch, v_bvbf)
        prog.wait("tensor", sch, sch.v)
        prog.wait("tensor", vch, vch.v)
        for c in range(NC_):
            for half in range(2):
                fn = lambda E_, c=c, half=half: E_.matmul(
                    PSB[0:1, 512 * half:512 * (half + 1)], bvp_bf[:, c:c + 1],
                    wo_bf[:, c, 512 * half:512 * (half + 1)],
                    start=(c == 0), stop=(c == NC_ - 1))
                if c == NC_ - 1 and half == 1:
                    prog.add("tensor", lambda E_, f=fn: f(E_).then_inc(s_t.h, 1))
                    s_t.inc()
                else:
                    prog.add("tensor", fn)
        t_bo = s_t.v
        prog.wait("vector", s_t, t_bo)
        prog.wait("vector", sch, s_garow)
        prog.wait("vector", s_cld, s_cld.v)
        vdep(lambda E_: E_.tensor_add(rowA[:], PSB[0:1, 0:1024], rowA[:]))
        prog.wait("vector", sch, s_gabc)
        vdep(lambda E_: E_.tensor_mul(rowA[:], rowA[:], tmp[0:1, 0:1024]))
        v_rowA = vch.v
        prog.wait("vector", sch, s_gabc)
        for c in range(NC_):
            vop(lambda E_, c=c: E_.tensor_mul(wo_bf[:, c, :], wo_bf[:, c, :], tmp[:, 0:1024]))
        v_wo = vch.v

        def build_bc_sbuf(dst, mcol):
            prog.wait("tensor", vch, vch.v)
            for c in range(NC_):
                fn = lambda E_, c=c, m=mcol: E_.transpose(
                    PSC[0:1, 128 * c:128 * (c + 1)], mod_sb[:, m + c:m + c + 1], id_sb[:])
                if c == NC_ - 1:
                    prog.add("tensor", lambda E_, f=fn: f(E_).then_inc(s_t.h, 1))
                    s_t.inc()
                else:
                    prog.add("tensor", fn)
            prog.wait("scalar", s_t, s_t.v)
            s_row = sop(lambda E_: E_.activation(rowB[:], PSC[0:1, 0:1024], AF.Copy))
            prog.wait("tensor", sch, s_row)
            for half in range(2):
                fn = lambda E_, half=half: E_.matmul(
                    PSC[:, 512 * half:512 * (half + 1)], ones_sb[:],
                    rowB[0:1, 512 * half:512 * (half + 1)], start=True, stop=True)
                if half == 1:
                    prog.add("tensor", lambda E_, f=fn: f(E_).then_inc(s_t.h, 1))
                    s_t.inc()
                else:
                    prog.add("tensor", fn)
            prog.wait("vector", s_t, s_t.v)
            vop(lambda E_, dst=dst: E_.tensor_copy(dst[:], PSC[:, 0:1024]))
            return vch.v

        def ln2_aggr(a, b):
            w = b - a
            vdep(lambda E_: E_.tensor_tensor(means4[:, a:b], bnall[:, a:b, :, 1], bnall[:, a:b, :, 4], ALU.add))
            vdep(lambda E_: E_.tensor_tensor(m2s4[:, a:b], bnall[:, a:b, :, 2], bnall[:, a:b, :, 5], ALU.add))
            sqv = tmp[:, 832:864].rearrange("p (t u) -> p t u", u=2)
            sqv2 = tmp[:, 864:896].rearrange("p (t u) -> p t u", u=2)
            vdep(lambda E_: E_.tensor_tensor(sqv[:, a:b], bnall[:, a:b, :, 1], bnall[:, a:b, :, 1], ALU.mult))
            vdep(lambda E_: E_.tensor_tensor(sqv2[:, a:b], bnall[:, a:b, :, 4], bnall[:, a:b, :, 4], ALU.mult))
            vdep(lambda E_: E_.tensor_add(sqv[:, a:b], sqv[:, a:b], sqv2[:, a:b]))
            vdep(lambda E_: E_.reduce_sum(tmp[:, 704 + a:704 + b], means4[:, a:b], AX.X))
            vdep(lambda E_: E_.reduce_sum(tmp[:, 720 + a:720 + b], m2s4[:, a:b], AX.X))
            vdep(lambda E_: E_.reduce_sum(tmp[:, 736 + a:736 + b], sqv[:, a:b], AX.X))
            vdep(lambda E_: E_.tensor_scalar_mul(stats[:, 32 + a:32 + b], tmp[:, 704 + a:704 + b], 0.25))
            vdep(lambda E_: E_.tensor_mul(tmp[:, 752 + a:752 + b], stats[:, 32 + a:32 + b], stats[:, 32 + a:32 + b]))
            vdep(lambda E_: E_.tensor_scalar_mul(tmp[:, 832 + a:832 + b], tmp[:, 736 + a:736 + b], 0.25))
            vdep(lambda E_: E_.scalar_tensor_tensor(
                tmp[:, 848 + a:848 + b], tmp[:, 720 + a:720 + b], 1.0 / 1024.0, tmp[:, 832 + a:832 + b], ALU.mult, ALU.add))
            vv = vdep(lambda E_: E_.tensor_sub(tmp[:, a:b], tmp[:, 848 + a:848 + b], tmp[:, 752 + a:752 + b]))
            prog.wait("scalar", vch, vv)
            ss = sdep(lambda E_, a=a, b=b: E_.activation(tmp[:, 16 + a:16 + b], tmp[:, a:b], AF.Sqrt, bias=smalls[:, 4:5]))
            prog.wait("vector", sch, ss)
            vop(lambda E_: E_.reciprocal(stats[:, 48 + a:48 + b], tmp[:, 16 + a:16 + b]))
            vdep(lambda E_: E_.scalar_tensor_tensor(
                umax_acc[:, a:b], stats[:, 32 + a:32 + b], -1.0, stats[:, 48 + a:48 + b], ALU.mult, ALU.mult))
            return vch.v

        agg_v = {}
        maxred_v = {}
        max_jobs = []

        def emit_max_tile(t):
            pb = h2scr[:] if t % 2 == 0 else PSB[:, 0:1024]
            if t >= 2:
                prog.wait("scalar", vch, maxred_v[t - 2])
            else:
                prog.wait("scalar", vch, v_rowA)
            prog.wait("scalar", vch, agg_v[t // 4])
            su = sdep(lambda E_, t=t, pb=pb: E_.activation(
                pb, x_sb[:, t, :], AF.Identity,
                scale=stats[:, 48 + t:49 + t], bias=umax_acc[:, t:t + 1]))
            prog.wait("vector", sch, su)
            vdep(lambda E_, pb=pb: E_.tensor_mul(pb, pb, cb1s_bc[:]))
            vdep(lambda E_, pb=pb: E_.tensor_add(pb, pb, sbs_bc[:]))
            maxred_v[t] = vdep(lambda E_, t=t, pb=pb: E_.tensor_reduce(
                hmax_acc[:, t:t + 1], pb, AX.X, ALU.max, apply_absolute_value=True))

        res_v = {}
        prog.wait("tensor", s_move, s_move.v)
        v_bc0 = None
        for i in range(NT):
            ps = (PS0, PS1)[i % 2]
            if i == 0:
                prog.wait("tensor", vch, v_wo)
                build_bc_sbuf(cb1s_bc, 72)
            if i >= 2:
                prog.wait("tensor", vch, res_v[i - 2])
            for half in range(2):
                fn = lambda E_, half=half, ps=ps: E_.matmul(
                    ps[:, 512 * half:512 * (half + 1)], ones_sb[:],
                    rowA[0:1, 512 * half:512 * (half + 1)], start=True, stop=False)
                prog.add("tensor", fn)
            for c in range(NC_):
                for half in range(2):
                    fn = lambda E_, c=c, half=half, i=i, ps=ps: E_.matmul(
                        ps[:, 512 * half:512 * (half + 1)], ctxv[:, c, 128 * i:128 * (i + 1)],
                        wo_bf[:, c, 512 * half:512 * (half + 1)],
                        start=False, stop=(c == NC_ - 1))
                    if c == NC_ - 1 and half == 1:
                        prog.add("tensor", lambda E_, f=fn: f(E_).then_inc(s_t.h, 1))
                        s_t.inc()
                    else:
                        prog.add("tensor", fn)
            t_o = s_t.v
            t_o_last = t_o
            if i == 0:
                build_bc_sbuf(sbs_bc, 24)
            prog.wait("vector", s_t, t_o)
            vop(lambda E_, i=i, ps=ps: E_.scalar_tensor_tensor(
                x_sb[:, i, :], ps[:, 0:1024], 1.0, x_sb[:, i, :], ALU.mult, ALU.add))
            res_v[i] = vch.v
            vop(lambda E_, i=i: E_.bn_stats(tmp[:, 512 + 16 * i:518 + 16 * i], x_sb[:, i, 0:512]))
            vop(lambda E_, i=i: E_.bn_stats(tmp[:, 520 + 16 * i:526 + 16 * i], x_sb[:, i, 512:1024]))
            if i % 4 == 3:
                g4 = i // 4
                agg_v[g4] = ln2_aggr(4 * g4, 4 * g4 + 4)
                max_jobs.extend(range(4 * g4, 4 * g4 + 4))
            if max_jobs:
                emit_max_tile(max_jobs.pop(0))
        while max_jobs:
            emit_max_tile(max_jobs.pop(0))

        if "x1" in dumps:
            ap = dump("x1", [S, D], F32)
            prog.wait("gpsimd", vch, vch.v)
            prog.add("gpsimd", lambda E_, ap=ap: E_.dma_start(
                out=ap.rearrange("(t p) d -> p t d", p=128), in_=x_sb[:]).then_inc(s_fin.h, 16))
            s_fin.inc(16)
        if "ctx" in dumps:
            ap = dump("ctx", [128, NC_ * S], BF16)
            prog.wait("gpsimd", vch, vch.v)
            prog.wait("gpsimd", s_move, s_move.v)
            prog.add("gpsimd", lambda E_, ap=ap: E_.dma_start(out=ap, in_=htb[:]).then_inc(s_fin.h, 16))
            s_fin.inc(16)

        if stage <= 3:
            prog.wait("sync", vch, vch.v)
            prog.wait("sync", sch, sch.v)
            prog.add("sync", lambda E_: E_.dma_start(
                out=out_hbm.rearrange("(t p) d -> p t d", p=128), in_=x_sb[:]).then_inc(s_fin.h, 16))
            s_fin.inc(16)
            prog.wait("sync", s_fin, s_fin.v)
            prog.wait("sync", s_cld, s_cld.v)
            emit(nc, prog)
            return nc

        gch = sm("gch")
        s_cc = sm("cc")
        s_xqt = sm("xqt")
        s_ffl = sm("ffl")
        s_uio = sm("uio")

        emit_nat(len(nat_jobs))
        g_nat_done = nat_red[len(nat_jobs) - 1]

        v_hmax = vdep(lambda E_: E_.tensor_reduce(hmax_acc[:, 16:17], hmax_acc[:, 0:16], AX.X, ALU.max))
        prog.wait("tensor", vch, v_hmax)
        fn = lambda E_: E_.transpose(PSB[0:1, 768:896], hmax_acc[:, 16:17], id_sb[:])
        prog.add("tensor", lambda E_, f=fn: f(E_).then_inc(s_t.h, 1))
        s_t.inc()
        prog.wait("vector", s_t, s_t.v)
        v_lmax = vdep(lambda E_: E_.tensor_reduce(csr[0:1, 10:11], PSB[0:1, 768:896], AX.X, ALU.max))
        prog.wait("gpsimd", vch, v_lmax)
        prog.add("gpsimd", lambda E_: E_.dma_start(out=cc1_in.ap(), in_=csr[0:1, 8:16]).then_inc(s_cc.h, 16))
        s_cc.inc(16)
        prog.wait("gpsimd", s_cc, s_cc.v)
        prog.add("gpsimd", lambda E_: E_.collective_compute(
            "AllReduce", ALU.max, replica_groups=[list(range(8))],
            ins=[cc1_in.ap()], outs=[cc1_out.ap()]).then_inc(s_cc.h, 1))
        s_cc.inc(1)
        prog.wait("gpsimd", s_cc, s_cc.v)
        prog.add("gpsimd", lambda E_: E_.dma_start(out=csr[0:1, 11:12], in_=cc1_out.ap()[0:1, 2:3]).then_inc(s_cc.h, 16))
        s_cc.inc(16)
        cc1_done = s_cc.v

        prog.wait("vector", vch, g_nat_done)
        vdep(lambda E_: E_.reciprocal(mod_sb[:, 96:104], swd_sb[:]))
        vdep(lambda E_: E_.tensor_scalar_mul(mod_sb[:, 96:104], mod_sb[:, 96:104], 127.0))
        vdep(lambda E_: E_.tensor_mul(mod_sb[:, 104:112], swd_sb[:], mod_sb[:, 80:88]))
        v_gsw = vdep(lambda E_: E_.tensor_scalar_mul(mod_sb[:, 104:112], mod_sb[:, 104:112], 1.0 / 127.0))

        def build_bc_psc(colrange):
            prog.wait("tensor", vch, vch.v)
            for c in range(NC_):
                fn = lambda E_, c=c, m=colrange: E_.transpose(
                    PS1[0:1, 128 * c:128 * (c + 1)], mod_sb[:, m + c:m + c + 1], id_sb[:])
                if c == NC_ - 1:
                    prog.add("tensor", lambda E_, f=fn: f(E_).then_inc(s_t.h, 1))
                    s_t.inc()
                else:
                    prog.add("tensor", fn)
            prog.wait("scalar", s_t, s_t.v)
            s_row = sop(lambda E_: E_.activation(rowB[:], PS1[0:1, 0:1024], AF.Copy))
            prog.wait("tensor", sch, s_row)
            for half in range(2):
                fn = lambda E_, half=half: E_.matmul(
                    PSC[:, 512 * half:512 * (half + 1)], ones_sb[:],
                    rowB[0:1, 512 * half:512 * (half + 1)], start=True, stop=True)
                if half == 1:
                    prog.add("tensor", lambda E_, f=fn: f(E_).then_inc(s_t.h, 1))
                    s_t.inc()
                else:
                    prog.add("tensor", fn)
            return s_t.v

        t_psc = build_bc_psc(96)
        wdv = wdT_in.rearrange("(a p) h -> p a h", p=128)
        prog.wait("vector", s_t, t_o_last)
        D1_HEAD = 6
        wd_q_v = []
        xq_done = [0]

        def emit_d1(icq):
            if icq < D1_HEAD:
                stg = wuT_s[:, icq % 2, :, :].rearrange("p c f -> p (c f)")
            else:
                stg = d1stg[:, icq % 2, :]
            if icq < 2:
                prog.wait("sync", s_t, st_avlast[7])
            if icq >= 2:
                prog.wait("sync", vch, wd_q_v[icq - 2])
            prog.add("sync", lambda E_, stg=stg, icq=icq: E_.dma_start(
                out=stg, in_=wdv[:, icq, :]).then_inc(s_ffl.h, 16))
            s_ffl.inc(16)
            prog.wait("vector", s_ffl, s_ffl.v)
            if icq == 0:
                prog.wait("vector", s_t, t_psc)
            vop(lambda E_, stg=stg: E_.tensor_mul(tmp[:, 0:1024], stg, PSC[:, 0:1024]))
            wd_q_v.append(vdep(lambda E_, icq=icq: E_.tensor_scalar(
                wd_pre[:, icq, :], tmp[:, 0:1024], RND, RND, ALU.add, ALU.subtract)))

        for icq in range(D1_HEAD):
            emit_d1(icq)
        d1_head_v = vch.v

        prog.wait("scalar", s_cc, cc1_done)
        s_sig = sop(lambda E_: E_.activation(csr[0:1, 12:13], csr[0:1, 8:9], AF.Sigmoid))
        prog.wait("vector", sch, s_sig)
        vdep(lambda E_: E_.tensor_mul(csr[0:1, 13:14], csr[0:1, 11:12], csr[0:1, 12:13]))
        vdep(lambda E_: E_.tensor_scalar_mul(csr[0:1, 13:14], csr[0:1, 13:14], 1.0 / 127.0))
        vdep(lambda E_: E_.reciprocal(csr[0:1, 14:15], csr[0:1, 13:14]))
        v_rs = vch.v
        prog.wait("tensor", vch, v_rs)
        fn = lambda E_: E_.matmul(PSB[:, 896:897], ones_sb[:], csr[0:1, 14:15], start=True, stop=True)
        prog.add("tensor", lambda E_, f=fn: f(E_).then_inc(s_t.h, 1))
        s_t.inc()
        fn = lambda E_: E_.matmul(PSB[:, 897:898], ones_sb[:], csr[0:1, 13:14], start=True, stop=True)
        prog.add("tensor", lambda E_, f=fn: f(E_).then_inc(s_t.h, 1))
        s_t.inc()
        t_rs = s_t.v
        prog.wait("vector", s_t, t_rs)
        vdep(lambda E_: E_.tensor_copy(smalls[:, 1:2], PSB[:, 896:897]))
        v_rspp = vdep(lambda E_: E_.tensor_copy(smalls[:, 0:1], PSB[:, 897:898]))
        vdep(lambda E_: E_.tensor_scalar_mul(cb1s_bc[:], cb1s_bc[:], smalls[:, 1:2]))
        v_cbs = vdep(lambda E_: E_.tensor_scalar_mul(sbs_bc[:], sbs_bc[:], smalls[:, 1:2]))
        prog.wait("vector", vch, g_nat_done)
        vdep(lambda E_, : E_.tensor_scalar(sswg_sb[:], swg_sb[:], smalls[:, 0:1], 1.0 / 127.0, ALU.mult, ALU.mult))
        vdep(lambda E_: E_.tensor_scalar(sswu_sb[:], swu_sb[:], smalls[:, 0:1], 1.0 / 127.0, ALU.mult, ALU.mult))
        vdep(lambda E_: E_.reciprocal(swg_sb[:], swg_sb[:]))
        vdep(lambda E_: E_.tensor_scalar_mul(swg_sb[:], swg_sb[:], 127.0))
        vdep(lambda E_: E_.reciprocal(swu_sb[:], swu_sb[:]))
        vdep(lambda E_: E_.tensor_scalar_mul(swu_sb[:], swu_sb[:], 127.0))
        v_rsw = vch.v

        xq_v = []
        xqt_vals = []
        s_u_v = []
        prog.wait("scalar", vch, d1_head_v)
        prog.wait("scalar", vch, v_cbs)
        for i in range(NT):
            pb = tmp[:, 0:1024] if i % 2 == 0 else PS0[:, 0:1024]
            if i >= 2:
                prog.wait("scalar", vch, xq_v[i - 2])
            s_u_v.append(sdep(lambda E_, i=i, pb=pb: E_.activation(
                pb, x_sb[:, i, :], AF.Identity,
                scale=stats[:, 48 + i:49 + i], bias=umax_acc[:, i:i + 1])))
            if i >= 2:
                prog.wait("vector", s_xqt, xqt_vals[i - 2])
            prog.wait("vector", sch, s_u_v[i])
            vdep(lambda E_, pb=pb: E_.tensor_mul(pb, pb, cb1s_bc[:]))
            vdep(lambda E_, pb=pb: E_.tensor_add(pb, pb, sbs_bc[:]))
            vdep(lambda E_, pb=pb: E_.tensor_scalar(pb, pb, RND, RND, ALU.add, ALU.subtract))
            xq_v.append(vdep(lambda E_, i=i, pb=pb: E_.tensor_scalar(
                xqtile[:, i % 2, :], pb, 127.0, -128.0, ALU.min, ALU.max)))
            prog.wait("scalar", vch, xq_v[i])
            prog.add("scalar", lambda E_, i=i: E_.dma_start_transpose(
                out=hf[:, :, 128 * i:128 * (i + 1)],
                in_=xqtile[:, i % 2, :].rearrange("p (c f) -> p c f", c=NC_)).then_inc(s_xqt.h, 16))
            xqt_vals.append(s_xqt.inc(16))
        XQT_ALL = s_xqt.v
        xq_done[0] = vch.v

        if "xq" in dumps:
            ap = dump("xq", [128, NC_ * S], BF16)
            prog.wait("gpsimd", s_xqt, XQT_ALL)
            prog.add("gpsimd", lambda E_, ap=ap: E_.dma_start(out=ap, in_=hfb[:]).then_inc(s_fin.h, 16))
            s_fin.inc(16)
        if "scl" in dumps:
            ap = dump("scl", [128, 16], F32)
            prog.wait("gpsimd", vch, v_cbs)
            prog.add("gpsimd", lambda E_, ap=ap: E_.dma_start(out=ap, in_=smalls[:]).then_inc(s_fin.h, 16))
            s_fin.inc(16)
        if "csr" in dumps:
            ap = dump("csr", [1, 16], F32)
            prog.wait("gpsimd", vch, v_cbs)
            prog.add("gpsimd", lambda E_, ap=ap: E_.dma_start(out=ap, in_=csr[:]).then_inc(s_fin.h, 16))
            s_fin.inc(16)
        if "hacc" in dumps:
            ap = dump("hacc", [128, 32], F32)
            prog.wait("gpsimd", vch, v_cbs)
            prog.add("gpsimd", lambda E_, ap=ap: E_.dma_start(out=ap, in_=hmax_acc[:]).then_inc(s_fin.h, 16))
            s_fin.inc(16)
        if "cbs" in dumps:
            ap = dump("cbs", [128, 2 * D], F32)
            prog.wait("gpsimd", vch, v_cbs)
            prog.add("gpsimd", lambda E_, ap=ap: E_.dma_start(out=ap[:, 0:D], in_=cb1s_bc[:]).then_inc(s_fin.h, 16))
            prog.add("gpsimd", lambda E_, ap=ap: E_.dma_start(out=ap[:, D:2 * D], in_=sbs_bc[:]).then_inc(s_fin.h, 16))
            s_fin.inc(32)

        if stage <= 4:
            prog.wait("sync", vch, vch.v)
            prog.wait("sync", sch, sch.v)
            prog.wait("sync", s_xqt, XQT_ALL)
            prog.add("sync", lambda E_: E_.dma_start(
                out=out_hbm.rearrange("(t p) d -> p t d", p=128), in_=x_sb[:]).then_inc(s_fin.h, 16))
            s_fin.inc(16)
            prog.wait("sync", s_fin, s_fin.v)
            prog.wait("sync", s_cld, s_cld.v)
            emit(nc, prog)
            return nc

        if stage == 5 and "wdp" in dumps:
            ap = dump("wdp", [128, NIC * D], BF16)
            prog.wait("gpsimd", vch, v_wd)
            prog.add("gpsimd", lambda E_, ap=ap: E_.dma_start(
                out=ap, in_=wd_pre[:].rearrange("p a b -> p (a b)")).then_inc(s_fin.h, 16))
            s_fin.inc(16)

        wgv = wgT_in.rearrange("(c p) i -> p c i", p=128)
        wuv = wuT_in.rearrange("(c p) i -> p c i", p=128)
        uo_v = {}
        q_free_v = {}
        t_gu = {}
        ev_g = {}
        u_v = {}
        t_bcq_prev = [0]

        def gate_prep(ic):
            bufp = ic % 2
            r0 = 256 if bufp == 0 else 768
            bc0 = 0 if bufp == 0 else 512
            if ic >= 2:
                prog.wait("sync", vch, q_free_v[ic - 2])
            for (wsl, wv_) in ((wgT_s, wgv), (wuT_s, wuv)):
                prog.add("sync", lambda E_, w=wsl, v=wv_, ic=ic, bufp=bufp: E_.dma_start(
                    out=w[:, bufp, :, :], in_=v[:, :, 128 * ic:128 * (ic + 1)]).then_inc(s_ffl.h, 16))
                s_ffl.inc(16)
            ld_v = s_ffl.v
            prog.wait("tensor", vch, v_rsw)
            if ic >= 2:
                prog.wait("tensor", vch, q_free_v[ic - 2])
            for k, scol in enumerate((swg_sb, swu_sb)):
                fn = lambda E_, co=r0 + 128 * k, s=scol, ic=ic: E_.transpose(
                    PSB[0:1, co:co + 128], s[:, ic:ic + 1], id_sb[:])
                prog.add("tensor", lambda E_, f=fn: f(E_).then_inc(s_t.h, 1))
                s_t.inc()
            t_trs = s_t.v
            prog.wait("scalar", s_t, t_trs)
            prog.wait("scalar", s_t, t_bcq_prev[0])
            sop(lambda E_, r0=r0: E_.activation(rowD[0:1, 0:128], PSB[0:1, r0:r0 + 128], AF.Copy))
            s_rD = sop(lambda E_, r0=r0: E_.activation(rowB[0:1, 0:128], PSB[0:1, r0 + 128:r0 + 256], AF.Copy))
            prog.wait("tensor", sch, s_rD)
            for k, rsrc in enumerate((rowD[0:1, 0:128], rowB[0:1, 0:128])):
                fn = lambda E_, co=bc0 + 128 * k, r=rsrc: E_.matmul(
                    PSB[:, co:co + 128], ones_sb[:], r, start=True, stop=True)
                prog.add("tensor", lambda E_, f=fn: f(E_).then_inc(s_t.h, 1))
                s_t.inc()
            t_bcq_prev[0] = s_t.v
            prog.wait("vector", s_ffl, ld_v)
            prog.wait("vector", s_t, t_bcq_prev[0])
            if (ic - 2, 3) in t_gu:
                prog.wait("vector", s_t, t_gu[(ic - 2, 3)])
            for (wsl, qd, co) in ((wgT_s, wgq, bc0), (wuT_s, wuq, bc0 + 128)):
                for hc in range(NC_):
                    vop(lambda E_, w=wsl, bufp=bufp, hc=hc, co=co: E_.tensor_mul(
                        tmp[:, 0:128], w[:, bufp, hc, :], PSB[:, co:co + 128]))
                    vdep(lambda E_, q=qd, bufp=bufp, hc=hc: E_.tensor_scalar(
                        q[:, bufp, hc, :], tmp[:, 0:128], RND, RND, ALU.add, ALU.subtract))
            q_free_v[ic] = vch.v

        def gate_mms(ic):
            bufp = ic % 2
            prog.wait("tensor", vch, q_free_v[ic])
            prog.wait("tensor", s_xqt, XQT_ALL)
            for tc in range(4):
                ps = (PS0, PS1)[tc % 2]
                prev = (ic, tc - 2) if tc >= 2 else (ic - 1, tc + 2)
                if prev in ev_g:
                    prog.wait("tensor", sch, ev_g[prev])
                for (qd, cols) in ((wgq, (0, 512)), (wuq, (512, 1024))):
                    for c in range(NC_):
                        fn = lambda E_, q=qd, bufp=bufp, c=c, tc=tc, ps=ps, cols=cols: E_.matmul(
                            ps[:, cols[0]:cols[1]], q[:, bufp, c, :],
                            hf[:, c, 512 * tc:512 * (tc + 1)],
                            start=(c == 0), stop=(c == NC_ - 1))
                        if c == NC_ - 1:
                            prog.add("tensor", lambda E_, f=fn: f(E_).then_inc(s_t.h, 1))
                            s_t.inc()
                        else:
                            prog.add("tensor", fn)
                t_gu[(ic, tc)] = s_t.v
                prog.wait("scalar", s_t, t_gu[(ic, tc)])
                prevsg = (ic, tc - 2) if tc >= 2 else (ic - 1, tc + 2)
                if prevsg in u_v:
                    prog.wait("scalar", vch, u_v[prevsg])
                sop(lambda E_, ic=ic, tc=tc, ps=ps: E_.activation(
                    sg_sb[:, tc % 2, :], ps[:, 0:512], AF.Silu, scale=sswg_sb[:, ic:ic + 1]))
                ev = sop(lambda E_, ic=ic, tc=tc, ps=ps: E_.activation(
                    up_sb[:, tc % 2, :], ps[:, 512:1024], AF.Identity, scale=sswu_sb[:, ic:ic + 1]))
                ev_g[(ic, tc)] = ev
                prog.wait("vector", sch, ev)
                prevu = (ic, tc - 2) if tc >= 2 else (ic - 1, tc + 2)
                if prevu in uo_v:
                    prog.wait("vector", s_uio, uo_v[prevu])
                vop(lambda E_, tc=tc: E_.tensor_mul(u_sb[:, tc % 2, :], sg_sb[:, tc % 2, :], up_sb[:, tc % 2, :]))
                u_v[(ic, tc)] = vdep(lambda E_, ic=ic, tc=tc: E_.tensor_reduce(
                    umax_acc[:, 4 * ic + tc:4 * ic + tc + 1], u_sb[:, tc % 2, :],
                    AX.X, ALU.max, apply_absolute_value=True))
                prog.wait("sync", vch, u_v[(ic, tc)])
                prog.add("sync", lambda E_, ic=ic, tc=tc: E_.dma_start(
                    out=u_scr[ic, :, 512 * tc:512 * (tc + 1)], in_=u_sb[:, tc % 2, :]).then_inc(s_uio.h, 16))
                uo_v[(ic, tc)] = s_uio.inc(16)

        prog.wait("sync", vch, d1_head_v)
        gate_prep(0)
        for ic in range(NIC):
            if ic + 1 < NIC:
                gate_prep(ic + 1)
            gate_mms(ic)
            if D1_HEAD + ic < NIC:
                emit_d1(D1_HEAD + ic)
        U_OUT_ALL = s_uio.v

        if "ffdbg" in dumps:
            ap = dump("ffdbg", [128, 128], F32)
            ap2 = dump("ffq", [128, 4 * NC_ * 128], BF16)
            prog.wait("gpsimd", vch, vch.v)
            prog.wait("gpsimd", sch, sch.v)
            prog.add("gpsimd", lambda E_, ap=ap: E_.dma_start(out=ap[:, 0:22], in_=sswg_sb[:]).then_inc(s_fin.h, 16))
            prog.add("gpsimd", lambda E_, ap=ap: E_.dma_start(out=ap[:, 22:44], in_=sswu_sb[:]).then_inc(s_fin.h, 16))
            prog.add("gpsimd", lambda E_, ap=ap: E_.dma_start(out=ap[:, 44:66], in_=swg_sb[:]).then_inc(s_fin.h, 16))
            prog.add("gpsimd", lambda E_, ap=ap: E_.dma_start(out=ap[:, 66:67], in_=smalls[:, 0:1], allow_slow_non_contiguous=True).then_inc(s_fin.h, 16))
            s_fin.inc(64)
            prog.add("gpsimd", lambda E_, ap=ap2: E_.dma_start(
                out=ap[:, 0:NC_ * 128], in_=wgq[:, 0, :, :].rearrange("p c f -> p (c f)")).then_inc(s_fin.h, 16))
            prog.add("gpsimd", lambda E_, ap=ap2: E_.dma_start(
                out=ap[:, NC_ * 128:2 * NC_ * 128], in_=wgq[:, 1, :, :].rearrange("p c f -> p (c f)")).then_inc(s_fin.h, 16))
            prog.add("gpsimd", lambda E_, ap=ap2: E_.dma_start(
                out=ap[:, 2 * NC_ * 128:3 * NC_ * 128], in_=wuq[:, 0, :, :].rearrange("p c f -> p (c f)")).then_inc(s_fin.h, 16))
            prog.add("gpsimd", lambda E_, ap=ap2: E_.dma_start(
                out=ap[:, 3 * NC_ * 128:4 * NC_ * 128], in_=wuq[:, 1, :, :].rearrange("p c f -> p (c f)")).then_inc(s_fin.h, 16))
            s_fin.inc(64)

        v_umax = vdep(lambda E_: E_.tensor_reduce(umax_acc[:, 88:89], umax_acc[:, 0:88], AX.X, ALU.max))
        prog.wait("tensor", vch, v_umax)
        fn = lambda E_: E_.transpose(PSB[0:1, 768:896], umax_acc[:, 88:89], id_sb[:])
        prog.add("tensor", lambda E_, f=fn: f(E_).then_inc(s_t.h, 1))
        s_t.inc()
        prog.wait("vector", s_t, s_t.v)
        v_lmax2 = vdep(lambda E_: E_.tensor_reduce(csr[0:1, 10:11], PSB[0:1, 768:896], AX.X, ALU.max))
        prog.wait("gpsimd", vch, v_lmax2)
        prog.add("gpsimd", lambda E_: E_.dma_start(out=cc2_in.ap(), in_=csr[0:1, 8:16]).then_inc(s_cc.h, 16))
        s_cc.inc(16)
        prog.wait("gpsimd", s_cc, s_cc.v)
        prog.add("gpsimd", lambda E_: E_.collective_compute(
            "AllReduce", ALU.max, replica_groups=[list(range(8))],
            ins=[cc2_in.ap()], outs=[cc2_out.ap()]).then_inc(s_cc.h, 1))
        s_cc.inc(1)
        prog.wait("gpsimd", s_cc, s_cc.v)
        prog.add("gpsimd", lambda E_: E_.dma_start(out=csr[0:1, 11:12], in_=cc2_out.ap()[0:1, 2:3]).then_inc(s_cc.h, 16))
        s_cc.inc(16)
        cc2_done = s_cc.v

        prog.wait("tensor", sch, sch.v)
        t_psc2 = build_bc_psc(104)
        for icq in range(NIC):
            if icq == 0:
                prog.wait("vector", s_t, t_psc2)
            vop(lambda E_, icq=icq: E_.tensor_mul(wd_pre[:, icq, :], wd_pre[:, icq, :], PSC[:, 0:1024]))
        v_wd = vch.v

        prog.wait("scalar", s_cc, cc2_done)
        s_sig2 = sop(lambda E_: E_.activation(csr[0:1, 12:13], csr[0:1, 9:10], AF.Sigmoid))
        prog.wait("vector", sch, s_sig2)
        vdep(lambda E_: E_.tensor_mul(csr[0:1, 13:14], csr[0:1, 11:12], csr[0:1, 12:13]))
        vdep(lambda E_: E_.tensor_scalar_mul(csr[0:1, 13:14], csr[0:1, 13:14], 1.0 / 127.0))
        vdep(lambda E_: E_.reciprocal(csr[0:1, 14:15], csr[0:1, 13:14]))
        v_rs2 = vch.v
        prog.wait("tensor", vch, v_rs2)
        fn = lambda E_: E_.matmul(PSB[:, 896:897], ones_sb[:], csr[0:1, 14:15], start=True, stop=True)
        prog.add("tensor", lambda E_, f=fn: f(E_).then_inc(s_t.h, 1))
        s_t.inc()
        fn = lambda E_: E_.matmul(PSB[:, 897:898], ones_sb[:], csr[0:1, 13:14], start=True, stop=True)
        prog.add("tensor", lambda E_, f=fn: f(E_).then_inc(s_t.h, 1))
        s_t.inc()
        t_rs2 = s_t.v
        prog.wait("vector", s_t, t_rs2)
        vdep(lambda E_: E_.tensor_copy(smalls[:, 3:4], PSB[:, 896:897]))
        v_s2pp = vdep(lambda E_: E_.tensor_copy(smalls[:, 2:3], PSB[:, 897:898]))

        if "s2row" in dumps:
            ap = dump("s2row", [1, 32], F32)
            prog.wait("gpsimd", vch, v_s2pp)
            prog.add("gpsimd", lambda E_, ap=ap: E_.dma_start(out=ap[0:1, 0:16], in_=csr[:]).then_inc(s_fin.h, 16))
            prog.add("gpsimd", lambda E_, ap=ap: E_.dma_start(out=ap[0:1, 16:32], in_=smalls[0:1, :]).then_inc(s_fin.h, 16))
            s_fin.inc(32)
        if "u0" in dumps:
            ap = dump("u0", [128, S], F32)
            prog.wait("gpsimd", s_uio, U_OUT_ALL)
            prog.add("gpsimd", lambda E_, ap=ap: E_.dma_start(out=ap, in_=u_scr[0]).then_inc(s_fin.h, 16))
            s_fin.inc(16)
        if "u21" in dumps:
            ap = dump("u21", [128, S], F32)
            prog.wait("gpsimd", s_uio, U_OUT_ALL)
            prog.add("gpsimd", lambda E_, ap=ap: E_.dma_start(out=ap, in_=u_scr[NIC - 1]).then_inc(s_fin.h, 16))
            s_fin.inc(16)
        if "umid" in dumps:
            ap = dump("umid", [128, 6 * S], F32)
            prog.wait("gpsimd", s_uio, U_OUT_ALL)
            for j, k in enumerate((5, 10, 14, 18, 19, 20)):
                prog.add("gpsimd", lambda E_, ap=ap, j=j, k=k: E_.dma_start(
                    out=ap[:, S * j:S * (j + 1)], in_=u_scr[k]).then_inc(s_fin.h, 16))
                s_fin.inc(16)

        dn_res = {}
        dn_t = {}
        qd_v = {}
        uv2 = u_scr.rearrange("a p t -> p a t")
        for i in range(NT):
            ps = (PS0, PS1)[i % 2]
            bufp = i % 2
            if i == 0:
                prog.wait("sync", s_uio, U_OUT_ALL)
            if i >= 2:
                prog.wait("sync", s_t, dn_t[i - 2])
            prog.add("sync", lambda E_, i=i, bufp=bufp: E_.dma_start(
                out=uin[:, bufp, :, :], in_=uv2[:, :, 128 * i:128 * (i + 1)]).then_inc(s_uio.h, 16))
            u_in = s_uio.inc(16)
            prog.wait("vector", s_uio, u_in)
            if i == 0:
                prog.wait("vector", vch, v_s2pp)
                prog.wait("vector", vch, v_wd)
            uflat = uin[:, bufp, :, :].rearrange("p a b -> p (a b)")
            hflat = h2q_s2[:, bufp, :, :].rearrange("p a b -> p (a b)")
            vdep(lambda E_, u=uflat: E_.tensor_scalar(u, u, smalls[:, 3:4], RND, ALU.mult, ALU.add))
            vdep(lambda E_, u=uflat: E_.tensor_scalar(u, u, RND, 127.0, ALU.subtract, ALU.min))
            qd_v[i] = vdep(lambda E_, u=uflat, h=hflat: E_.tensor_scalar(h, u, -128.0, None, ALU.max))
            prog.wait("tensor", vch, qd_v[i])
            if i >= 2:
                prog.wait("tensor", vch, dn_res[i - 2])
            for ic in range(NIC):
                for half in range(2):
                    fn = lambda E_, i=i, ic=ic, half=half, bufp=bufp, ps=ps: E_.matmul(
                        ps[:, 512 * half:512 * (half + 1)], h2q_s2[:, bufp, ic, :],
                        wd_pre[:, ic, 512 * half:512 * (half + 1)],
                        start=(ic == 0), stop=(ic == NIC - 1))
                    if ic == NIC - 1 and half == 1:
                        prog.add("tensor", lambda E_, f=fn: f(E_).then_inc(s_t.h, 1))
                        s_t.inc()
                    else:
                        prog.add("tensor", fn)
            dn_t[i] = s_t.v
            prog.wait("vector", s_t, dn_t[i])
            dn_res[i] = vop(lambda E_, i=i, ps=ps: E_.scalar_tensor_tensor(
                x_sb[:, i, :], ps[:, 0:1024], smalls[:, 2:3], x_sb[:, i, :], ALU.mult, ALU.add))

        prog.wait("sync", vch, dn_res[NT - 1])
        prog.add("sync", lambda E_: E_.dma_start(
            out=out_hbm.rearrange("(t p) d -> p t d", p=128), in_=x_sb[:]).then_inc(s_fin.h, 16))
        s_fin.inc(16)
        prog.wait("sync", s_fin, s_fin.v)
        prog.wait("sync", s_cld, s_cld.v)
        prog.wait("sync", s_move, s_move.v)
        prog.wait("sync", s_uio, s_uio.v)
        emit(nc, prog)
        return nc

    return nc


def emit(nc, prog):
    with nc.Block() as blk:
        @blk.sync
        def _(e):
            for fn in prog.ops["sync"]:
                fn(e)

        @blk.gpsimd
        def _(e):
            for fn in prog.ops["gpsimd"]:
                fn(e)

        @blk.tensor
        def _(e):
            for fn in prog.ops["tensor"]:
                fn(e)

        @blk.vector
        def _(e):
            for fn in prog.ops["vector"]:
                fn(e)

        @blk.scalar
        def _(e):
            for fn in prog.ops["scalar"]:
                fn(e)


def make_inputs(inputs):
    x = np.ascontiguousarray(inputs["L0_attn_embeddings"])
    c = np.ascontiguousarray(inputs["L0_c"])
    shared = {
        "wq": np.ascontiguousarray(inputs["L0_attn_Wq"]),
        "wk": np.ascontiguousarray(inputs["L0_attn_Wk"]),
        "wv": np.ascontiguousarray(inputs["L0_attn_Wv"]),
        "bq_f": np.ascontiguousarray(np.asarray(inputs["L0_attn_bq"]).reshape(8, 128).T),
        "bk_f": np.ascontiguousarray(np.asarray(inputs["L0_attn_bk"]).reshape(8, 128).T),
        "bv_f": np.ascontiguousarray(np.asarray(inputs["L0_attn_bv"]).reshape(8, 128).T),
        "bo_row": np.ascontiguousarray(np.asarray(inputs["L0_attn_bo"]).reshape(1, 1024)),
        "wo": np.ascontiguousarray(inputs["L0_attn_Wo"]),
        "wmod": np.ascontiguousarray(inputs["L0_wmod"]),
        "clipx": np.ascontiguousarray(np.asarray(inputs["L0_ffn_clip_x"]).reshape(1, 1)),
        "clipd": np.ascontiguousarray(np.asarray(inputs["L0_ffn_clip_down"]).reshape(1, 1)),
        "wg_nat": np.ascontiguousarray(inputs["L0_ffn_w_gate"]),
        "wu_nat": np.ascontiguousarray(inputs["L0_ffn_w_up"]),
        "wd_nat": np.ascontiguousarray(inputs["L0_ffn_w_down"]),
        "wgT": np.ascontiguousarray(np.asarray(inputs["L0_ffn_w_gate"]).T),
        "wuT": np.ascontiguousarray(np.asarray(inputs["L0_ffn_w_up"]).T),
        "wdT": np.ascontiguousarray(np.asarray(inputs["L0_ffn_w_down"]).T),
    }
    per_core = []
    for b in range(B):
        m = dict(shared)
        m["x"] = np.ascontiguousarray(x[b])
        m["c_f"] = np.ascontiguousarray(c[b].reshape(8, 128).T)
        per_core.append(m)
    return per_core


def kernel(**inputs):
    in_maps = make_inputs(inputs)
    nc = build_nc()
    res = run_bass_kernel_spmd(nc, in_maps, core_ids=list(range(B)))
    return np.stack([r["out"] for r in res.results], axis=0)


# version 22
# speedup vs baseline: 1.0042x; 1.0042x over previous
import numpy as np
from contextlib import ExitStack

import concourse.bass as bass
import concourse.mybir as mybir
from concourse.bass_utils import run_bass_kernel_spmd

F32 = mybir.dt.float32
BF16 = mybir.dt.bfloat16
AF = mybir.ActivationFunctionType
ALU = mybir.AluOpType
AX = mybir.AxisListType

RND = 1.5 * 2.0 ** 23
B, S, D, H, DK, I = 8, 2048, 1024, 16, 64, 2816
NT = S // 128
NC_ = D // 128
NIC = I // 128


class Sem:
    def __init__(self, ctx, nc, name):
        self.h = ctx.enter_context(nc.semaphore(name))
        self.v = 0

    def inc(self, k=1):
        self.v += k
        return self.v


class Prog:
    ENGS = ("sync", "gpsimd", "tensor", "vector", "scalar")

    def __init__(self):
        self.ops = {e: [] for e in self.ENGS}

    def add(self, eng, fn):
        self.ops[eng].append(fn)

    def wait(self, eng, sem, val=None):
        v = sem.v if val is None else val
        if v > 0:
            self.ops[eng].append(lambda E, s=sem.h, v=v: E.wait_ge(s, v))


def build_nc(stage=99, dumps=()):
    dumps = set(dumps)
    nc = bass.Bass("TRN2", target_bir_lowering=False, detect_race_conditions=False)

    def din(name, shape, dt=F32):
        return nc.dram_tensor(name, shape, dt, kind="ExternalInput").ap()

    x_in = din("x", [S, D])
    c_f_in = din("c_f", [128, NC_])
    wq_in = din("wq", [H, D, DK])
    wk_in = din("wk", [H, D, DK])
    wv_in = din("wv", [H, D, DK])
    bq_in = din("bq_f", [128, NC_])
    bk_in = din("bk_f", [128, NC_])
    bv_in = din("bv_f", [128, NC_])
    bo_in = din("bo_row", [1, D])
    wo_in = din("wo", [D, D])
    wmod_in = din("wmod", [D, 6 * D])
    clipx_in = din("clipx", [1, 1])
    clipd_in = din("clipd", [1, 1])
    wgn_in = din("wg_nat", [I, D])
    wun_in = din("wu_nat", [I, D])
    wdn_in = din("wd_nat", [D, I])
    wgT_in = din("wgT", [D, I])
    wuT_in = din("wuT", [D, I])
    wdT_in = din("wdT", [I, D])

    out_hbm = nc.dram_tensor("out", [S, D], F32, kind="ExternalOutput").ap()

    dump_aps = {}

    def dump(name, shape, dt=F32):
        if name in dumps:
            dump_aps[name] = nc.dram_tensor("d_" + name, shape, dt, kind="ExternalOutput").ap()
            return dump_aps[name]
        return None

    u_scr = nc.dram_tensor("u_scr", [NIC, 128, S], F32, kind="Internal").ap()
    h2q_scr = nc.dram_tensor("h2q_scr", [NIC, 128, S], BF16, kind="Internal").ap()
    cc1_in = nc.dram_tensor("cc1_in", [1, 8], F32, kind="Internal")
    cc1_out = nc.dram_tensor("cc1_out", [1, 8], F32, kind="Internal")
    cc2_in = nc.dram_tensor("cc2_in", [1, 8], F32, kind="Internal")
    cc2_out = nc.dram_tensor("cc2_out", [1, 8], F32, kind="Internal")

    ident = nc.inline_tensor(np.eye(128, dtype=np.float32), name="ident").ap()
    ones_row_c = nc.inline_tensor(np.ones((1, 128), dtype=np.float32), name="ones_row").ap()
    import ml_dtypes as _mld
    onescol_c = nc.inline_tensor(np.ones((128, 1), dtype=np.float32), name="onescol").ap()
    onescolb_c = nc.inline_tensor(np.ones((128, 1), dtype=_mld.bfloat16), name="onescolb").ap()
    mask_np = (np.arange(128)[None, :] >= np.arange(128)[:, None])
    mask_c = nc.inline_tensor(mask_np.astype(_mld.bfloat16), name="mask_c").ap()

    prog = Prog()

    with ExitStack() as ctx:
        E = ctx.enter_context

        x_sb = E(nc.sbuf_tensor("x_sb", [128, NT, D], F32))
        hfb = E(nc.sbuf_tensor("hfb", [128, NC_ * S], BF16))
        tmp = E(nc.sbuf_tensor("tmp", [128, 1024], F32))
        stats = E(nc.sbuf_tensor("stats", [128, 64], F32))
        mod_sb = E(nc.sbuf_tensor("mod_sb", [128, 112], F32))
        smalls = E(nc.sbuf_tensor("smalls", [128, 16], F32))
        hmax_acc = E(nc.sbuf_tensor("hmax_acc", [128, 32], F32))
        umax_acc = E(nc.sbuf_tensor("umax_acc", [128, 96], F32))
        swg_sb = E(nc.sbuf_tensor("swg_sb", [128, NIC], F32))
        swu_sb = E(nc.sbuf_tensor("swu_sb", [128, NIC], F32))
        swd_sb = E(nc.sbuf_tensor("swd_sb", [128, NC_], F32))
        sswg_sb = E(nc.sbuf_tensor("sswg_sb", [128, NIC], F32))
        sswu_sb = E(nc.sbuf_tensor("sswu_sb", [128, NIC], F32))
        c_sb = E(nc.sbuf_tensor("c_sb", [128, NC_], F32))
        silu_c = E(nc.sbuf_tensor("silu_c", [128, NC_], BF16))
        saca_bf = E(nc.sbuf_tensor("saca_bf", [128, NC_], BF16))
        bq_sb = E(nc.sbuf_tensor("bq_sb", [128, NC_], F32))
        bk_sb = E(nc.sbuf_tensor("bk_sb", [128, NC_], F32))
        bv_sb = E(nc.sbuf_tensor("bv_sb", [128, NC_], F32))
        bqp = E(nc.sbuf_tensor("bqp", [128, 4], F32))
        bvp_sb = E(nc.sbuf_tensor("bvp_sb", [128, NC_], F32))
        bvp_bf = E(nc.sbuf_tensor("bvp_bf", [128, NC_], BF16))
        negc_sb = E(nc.sbuf_tensor("negc_sb", [128, 2], F32))
        csr = E(nc.sbuf_tensor("csr", [1, 16], F32))
        rowA = E(nc.sbuf_tensor("rowA", [1, D], F32))
        rowB = E(nc.sbuf_tensor("rowB", [1, 1024], F32))
        rowD = E(nc.sbuf_tensor("rowD", [1, 128], F32))
        id_sb = E(nc.sbuf_tensor("id_sb", [128, 128], F32))
        ones_sb = E(nc.sbuf_tensor("ones_sb", [1, 128], F32))
        onescol_sb = E(nc.sbuf_tensor("onescol_sb", [128, 1], F32))
        onescol_bf = E(nc.sbuf_tensor("onescol_bf", [128, 1], BF16))
        mask_sb = E(nc.sbuf_tensor("mask_sb", [128, 128], BF16))

        qbase = nc.sbuf_base
        qtop = nc.sbuf_top

        def place(name, shape, dt, off):
            t = nc.alloc_sbuf_tensor_at(name, shape, dt, offset=qbase + off)
            nbytes = int(np.prod(shape[1:])) * mybir.dt.size(dt)
            assert qbase + off + nbytes <= qtop, (name, off, nbytes, qtop - qbase)
            return t

        KB = 1024
        htb = place("htb", [128, NT * D], BF16, 0)
        wmod_bf = place("wmod_bf", [128, NC_, 512], BF16, 33 * KB)
        wmod_bf2 = place("wmod_bf2", [128, NC_, 512], BF16, 41 * KB)
        wq_bf = place("wq_bf", [128, 2, NC_, 128], BF16, 33 * KB)
        wk_bf = place("wk_bf", [128, 2, NC_, 128], BF16, 37 * KB)
        wv_bf = place("wv_bf", [128, 2, NC_, 128], BF16, 41 * KB)
        qf = place("qf", [128, 2, S], BF16, 45 * KB)
        kf = place("kf", [128, 2, S], BF16, 53 * KB)
        wo_bf = place("wo_bf", [128, NC_, D], BF16, 76 * KB + 512)
        vt = place("vt", [128, 2, NT, 130], BF16, 61 * KB)
        cb1s_bc = place("cb1s_bc", [128, D], F32, 61 * KB)
        sbs_bc = place("sbs_bc", [128, D], F32, 65 * KB)
        sq_sb = place("sq_sb", [128, 512], BF16, 69 * KB + 256)
        sq2_sb = place("sq2_sb", [128, 512], BF16, 92 * KB + 512)
        p_sb = place("p_sb", [128, 2, 1024], BF16, 70 * KB + 512)
        ctx_stage = place("ctx_stage", [128, 1024], BF16, 74 * KB + 512)
        natw = place("natw", [128, 4, 1024], F32, 76 * KB + 512)
        xqtile = place("xqtile", [128, 2, 1024], BF16, 70 * KB + 512)
        h2scr = place("h2scr", [128, 1024], F32, 70 * KB + 512)
        d1stg = place("d1stg", [128, 2, 1024], F32, 80 * KB + 512)
        wd_pre = place("wd_pre", [128, NIC, D], BF16, 0)
        wgT_s = place("wgT_s", [128, 2, NC_, 128], F32, 44 * KB)
        wuT_s = place("wuT_s", [128, 2, NC_, 128], F32, 52 * KB)
        wgq = place("wgq", [128, 2, NC_, 128], BF16, 60 * KB)
        wuq = place("wuq", [128, 2, NC_, 128], BF16, 64 * KB)
        sg_sb = place("sg_sb", [128, 2, 512], F32, 68 * KB)
        up_sb = place("up_sb", [128, 2, 512], F32, 72 * KB)
        u_sb = place("u_sb", [128, 2, 512], F32, 76 * KB)
        uin = place("uin", [128, 2, NIC, 128], F32, 44 * KB)
        h2q_s2 = place("h2q_s2", [128, 2, NIC, 128], BF16, 67 * KB)
        assert qbase + 81 * KB <= qtop, (qbase, qtop)

        PS0 = E(nc.psum_tensor("PS0", [128, 1024], F32))
        PS1 = E(nc.psum_tensor("PS1", [128, 1024], F32))
        PSC = E(nc.psum_tensor("PSC", [128, 1024], F32))
        PSB = E(nc.psum_tensor("PSB", [128, 1024], F32))

        sm = lambda name: Sem(ctx, nc, name)
        s_xld = sm("xld")
        s_cld = sm("cld")
        s_wmod = sm("wmodld")
        s_mod = sm("mod")
        s_modv = sm("modv")
        s_stat = sm("stat")
        s_rstd = sm("rstd")
        s_h = sm("h")
        s_ht = sm("ht")
        s_silu = sm("silu")
        s_fin = sm("fin")
        vch = sm("vch")
        sch = sm("sch")

        def vop(fn, dep=False):
            if dep:
                prog.wait("vector", vch, vch.v)
            prog.add("vector", lambda E_, f=fn: f(E_).then_inc(vch.h, 1))
            return vch.inc()

        def vdep(fn):
            return vop(fn, dep=True)

        def sop(fn, dep=False):
            if dep:
                prog.wait("scalar", sch, sch.v)
            prog.add("scalar", lambda E_, f=fn: f(E_).then_inc(sch.h, 1))
            return sch.inc()

        def sdep(fn):
            return sop(fn, dep=True)

        xv = x_in.rearrange("(t p) d -> p t d", p=128)
        for q in range(8):
            prog.add("sync", lambda E_, q=q: E_.dma_start(
                out=x_sb[:, 2 * q:2 * q + 2, :], in_=xv[:, 2 * q:2 * q + 2, :]).then_inc(s_xld.h, 16))
            s_xld.inc(16)
        for ap_out, ap_in in ((id_sb[:], ident), (ones_sb[:], ones_row_c),
                              (onescol_sb[:], onescol_c), (c_sb[:], c_f_in),
                              (bq_sb[:], bq_in), (bk_sb[:], bk_in), (bv_sb[:], bv_in),
                              (rowA[:], bo_in), (csr[0:1, 8:9], clipx_in),
                              (csr[0:1, 9:10], clipd_in)):
            prog.add("sync", lambda E_, o=ap_out, i=ap_in: E_.dma_start(out=o, in_=i).then_inc(s_cld.h, 16))
            s_cld.inc(16)

        prog.add("sync", lambda E_: E_.dma_start(out=mask_sb[:], in_=mask_c).then_inc(s_cld.h, 16))
        s_cld.inc(16)
        prog.add("sync", lambda E_: E_.dma_start(out=onescol_bf[:], in_=onescolb_c).then_inc(s_cld.h, 16))
        s_cld.inc(16)
        wmv = wmod_in.rearrange("(c p) m -> p c m", p=128)
        wmod_ld_vals = []
        for ch in range(12):
            buf = wmod_bf if ch % 2 == 0 else wmod_bf2
            if ch >= 2:
                prog.wait("gpsimd", s_mod, ch - 1)
            prog.add("gpsimd", lambda E_, b=buf, ch=ch: E_.dma_start(
                out=b[:], in_=wmv[:, :, 512 * ch:512 * (ch + 1)]).then_inc(s_wmod.h, 16))
            wmod_ld_vals.append(s_wmod.inc(16))

        v_k = vop(lambda E_: E_.memset(smalls[:, 4:5], 1e-5))

        prog.wait("scalar", s_cld, s_cld.v)
        prog.add("scalar", lambda E_: E_.activation(silu_c[:], c_sb[:], AF.Silu).then_inc(s_silu.h, 1))
        s_silu.inc()

        prog.wait("tensor", s_silu, 1)
        for ch in range(12):
            buf = wmod_bf if ch % 2 == 0 else wmod_bf2
            prog.wait("tensor", s_wmod, wmod_ld_vals[ch])
            for j4 in range(4):
                j = ch * 4 + j4
                for c in range(NC_):
                    first, last = (c == 0), (c == NC_ - 1)
                    def mk(b=buf, j4=j4, c=c, j=j, first=first, last=last):
                        def f(E_):
                            return E_.matmul(PSB[:, j:j + 1], b[:, c, 128 * j4:128 * (j4 + 1)],
                                             silu_c[:, c:c + 1], start=first, stop=last)
                        return f
                    fn = mk()
                    if last:
                        prog.add("tensor", lambda E_, f=fn: f(E_).then_inc(s_mod.h, 1))
                    else:
                        prog.add("tensor", fn)
            s_mod.inc(4)

        prog.wait("vector", s_mod, 48)
        vop(lambda E_: E_.tensor_copy(mod_sb[:, 0:48], PSB[:, 0:48]))
        vdep(lambda E_: E_.tensor_scalar_add(mod_sb[:, 48:56], mod_sb[:, 8:16], 1.0))
        vdep(lambda E_: E_.tensor_scalar_add(mod_sb[:, 72:80], mod_sb[:, 32:40], 1.0))
        vdep(lambda E_: E_.reciprocal(mod_sb[:, 88:96], mod_sb[:, 48:56]))
        vdep(lambda E_: E_.tensor_mul(mod_sb[:, 56:64], mod_sb[:, 0:8], mod_sb[:, 88:96]))
        for (si, dst) in ((16, 64), (40, 80)):
            vdep(lambda E_, s=si: E_.tensor_scalar_mul(mod_sb[:, 96:104], mod_sb[:, s:s + 8], -1.0))
            vdep(lambda E_, s=si: E_.tensor_max(mod_sb[:, 104:112], mod_sb[:, s:s + 8], mod_sb[:, 96:104]))
            vdep(lambda E_: E_.tensor_scalar_max(mod_sb[:, 96:104], mod_sb[:, 104:112], 0.25))
            vdep(lambda E_, s=si: E_.tensor_scalar(
                mod_sb[:, 104:112], mod_sb[:, s:s + 8], 0.0, 2.0, ALU.is_ge, ALU.mult))
            vdep(lambda E_, d=dst: E_.scalar_tensor_tensor(
                mod_sb[:, d:d + 8], mod_sb[:, 104:112], -1.0, mod_sb[:, 96:104], ALU.add, ALU.mult))
        v_modv = vdep(lambda E_: E_.tensor_copy(saca_bf[:], mod_sb[:, 56:64]))

        ht = htb[:].rearrange("p (t d) -> p t d", t=NT)
        for i in range(NT):
            prog.wait("vector", s_xld, 16 * (i // 2 + 1))
            vop(lambda E_, i=i: E_.bn_stats(tmp[:, 512 + 16 * i:518 + 16 * i], x_sb[:, i, 0:512]))
            vop(lambda E_, i=i: E_.bn_stats(tmp[:, 520 + 16 * i:526 + 16 * i], x_sb[:, i, 512:1024]))
        bnall = tmp[:, 512:768].rearrange("p (t u k) -> p t u k", u=2, k=8)
        means4 = tmp[:, 768:800].rearrange("p (t u) -> p t u", u=2)
        m2s4 = tmp[:, 800:832].rearrange("p (t u) -> p t u", u=2)
        vdep(lambda E_: E_.tensor_tensor(means4, bnall[:, :, :, 1], bnall[:, :, :, 4], ALU.add))
        vdep(lambda E_: E_.tensor_tensor(m2s4, bnall[:, :, :, 2], bnall[:, :, :, 5], ALU.add))
        vdep(lambda E_: E_.tensor_tensor(
            tmp[:, 832:864].rearrange("p (t u) -> p t u", u=2),
            bnall[:, :, :, 1], bnall[:, :, :, 1], ALU.mult))
        vdep(lambda E_: E_.tensor_tensor(
            tmp[:, 864:896].rearrange("p (t u) -> p t u", u=2),
            bnall[:, :, :, 4], bnall[:, :, :, 4], ALU.mult))
        vdep(lambda E_: E_.tensor_add(
            tmp[:, 832:864].rearrange("p (t u) -> p t u", u=2),
            tmp[:, 832:864].rearrange("p (t u) -> p t u", u=2),
            tmp[:, 864:896].rearrange("p (t u) -> p t u", u=2)))
        vdep(lambda E_: E_.reduce_sum(tmp[:, 704:720], means4, AX.X))
        vdep(lambda E_: E_.reduce_sum(tmp[:, 720:736], m2s4, AX.X))
        vdep(lambda E_: E_.reduce_sum(tmp[:, 736:752], tmp[:, 832:864].rearrange("p (t u) -> p t u", u=2), AX.X))
        vdep(lambda E_: E_.tensor_scalar_mul(stats[:, 0:16], tmp[:, 704:720], 0.25))
        vdep(lambda E_: E_.tensor_mul(tmp[:, 752:768], stats[:, 0:16], stats[:, 0:16]))
        vdep(lambda E_: E_.tensor_scalar_mul(tmp[:, 832:848], tmp[:, 736:752], 0.25))
        vdep(lambda E_: E_.scalar_tensor_tensor(
            tmp[:, 848:864], tmp[:, 720:736], 1.0 / 1024.0, tmp[:, 832:848], ALU.mult, ALU.add))
        v_stat = vdep(lambda E_: E_.tensor_sub(tmp[:, 0:16], tmp[:, 848:864], tmp[:, 752:768]))
        prog.wait("scalar", vch, v_stat)
        s_sqrt = sop(lambda E_: E_.activation(tmp[:, 16:32], tmp[:, 0:16], AF.Sqrt, bias=smalls[:, 4:5]))
        prog.wait("vector", sch, s_sqrt)
        vop(lambda E_: E_.reciprocal(stats[:, 16:32], tmp[:, 16:32]))

        v_h = []
        first = True
        for i in range(NT):
            f = lambda E_, i=i: E_.tensor_scalar(
                ht[:, i, :], x_sb[:, i, :], stats[:, i:i + 1], stats[:, 16 + i:17 + i],
                ALU.subtract, ALU.mult)
            v_h.append(vop(f, dep=first))
            first = False

        hf = hfb[:].rearrange("p (c s) -> p c s", c=NC_)
        for i in range(NT):
            prog.wait("scalar", vch, v_h[i])
            prog.add("scalar", lambda E_, i=i: E_.dma_start_transpose(
                out=hf[:, :, 128 * i:128 * (i + 1)],
                in_=ht[:, i, :].rearrange("p (c f) -> p c f", c=NC_)).then_inc(s_ht.h, 16))
            s_ht.inc(16)

        if "h_f" in dumps:
            ap = dump("h_f", [128, NC_ * S], BF16)
            prog.wait("gpsimd", s_ht, s_ht.v)
            prog.add("gpsimd", lambda E_, ap=ap: E_.dma_start(out=ap, in_=hfb[:]).then_inc(s_fin.h, 16))
            s_fin.inc(16)
        if "mod" in dumps:
            ap = dump("mod", [128, 112], F32)
            prog.wait("gpsimd", vch, v_modv)
            prog.add("gpsimd", lambda E_, ap=ap: E_.dma_start(out=ap, in_=mod_sb[:]).then_inc(s_fin.h, 16))
            s_fin.inc(16)
        if "stats" in dumps:
            ap = dump("stats", [128, 64], F32)
            prog.wait("gpsimd", vch, vch.v)
            prog.add("gpsimd", lambda E_, ap=ap: E_.dma_start(out=ap, in_=stats[:]).then_inc(s_fin.h, 16))
            s_fin.inc(16)

        if stage <= 1:
            prog.wait("sync", s_ht, s_ht.v)
            prog.wait("sync", vch, v_modv)
            prog.add("sync", lambda E_: E_.dma_start(
                out=out_hbm.rearrange("(t p) d -> p t d", p=128), in_=x_sb[:]).then_inc(s_fin.h, 16))
            s_fin.inc(16)
            prog.wait("sync", s_fin, s_fin.v)
            prog.wait("sync", s_cld, s_cld.v)
            emit(nc, prog)
            return nc

        s_wld = sm("wld")
        s_t = sm("tmm")
        s_move = sm("cmove")

        ctxv = htb[:].rearrange("p (g s) -> p g s", g=NC_)
        HT_ALL = s_ht.v

        vt5 = vt[:].rearrange("p b t (h k) -> p b t h k", k=65)
        v_vones = vop(lambda E_: E_.memset(vt5[:, :, :, :, 64:65], 1.0))
        zrow = E(nc.sbuf_tensor("zrow", [1, 65], F32))
        v_zrow = vop(lambda E_: E_.memset(zrow[:], 0.0))

        npairs = 2 if stage == 2 else 8
        st_exp_last = [0, 0]
        st_ctx_free = 0
        sv_ctx_free = 0
        st_avlast = {}
        sv_vev_last = {}
        ss_qkev_last = {}
        sv_bq = {}
        q_evac_sch = {}
        exp_sch_prev_pair = 0
        vch_prev_pair = 0
        prev_iter_exp = 0
        prev_iter_avlast = 0
        psc_free = [0]

        wprep_done = {}
        wuse_done = {}
        negc_pend = [0]

        def load_pair(g):
            p = g % 2
            if g < 2:
                prog.wait("gpsimd", s_mod, 48)
            if g >= 2:
                prog.wait("gpsimd", s_t, wuse_done.get(g - 2, 0))
            for (wdst, wsrc) in ((wq_bf, wq_in), (wk_bf, wk_in), (wv_bf, wv_in)):
                for h2 in range(2):
                    prog.add("gpsimd", lambda E_, o=wdst, s=wsrc, p=p, g=g, h2=h2: E_.dma_start(
                        out=o[:, p, :, 64 * h2:64 * h2 + 64],
                        in_=s[2 * g + h2].rearrange("(c p) k -> p c k", p=128)).then_inc(s_wld.h, 16))
                    s_wld.inc(16)
            wld_done = s_wld.v
            prog.wait("vector", s_wld, wld_done)
            first = (g == 0)
            for wdst in (wq_bf, wk_bf, wv_bf):
                for c in range(NC_):
                    f = lambda E_, o=wdst, p=p, c=c: E_.tensor_scalar_mul(
                        o[:, p, c, :], o[:, p, c, :], mod_sb[:, 48 + c:49 + c])
                    vop(f, dep=first)
                    first = False
            wprep_done[g] = vch.v

        def qkv_pair(g):
            p = g % 2
            if g >= 2:
                prog.wait("tensor", s_t, st_avlast[g - 2])
            prog.wait("tensor", vch, wprep_done[g])
            if g >= 1:
                prog.wait("tensor", sch, exp_sch_prev_pair)
                prog.wait("tensor", vch, vch_prev_pair)
            for bi, wdst in enumerate((wq_bf, wk_bf, wv_bf)):
                for c in range(NC_):
                    fn = lambda E_, o=wdst, p=p, c=c, bi=bi: E_.matmul(
                        PS1[:, 768 + bi:769 + bi], o[:, p, c, :], saca_bf[:, c:c + 1],
                        start=(c == 0), stop=(c == NC_ - 1))
                    if c == NC_ - 1:
                        prog.add("tensor", lambda E_, f=fn: f(E_).then_inc(s_t.h, 1))
                        s_t.inc()
                    else:
                        prog.add("tensor", fn)
            t_bias = s_t.v
            prog.wait("vector", s_t, t_bias)
            vdep(lambda E_, p=p, g=g: E_.tensor_add(bqp[:, 2 * p:2 * p + 1], PS1[:, 768:769], bq_sb[:, g:g + 1]))
            vdep(lambda E_, p=p, g=g: E_.tensor_add(bqp[:, 2 * p + 1:2 * p + 2], PS1[:, 769:770], bk_sb[:, g:g + 1]))
            v_bq = vch.v
            vdep(lambda E_, g=g: E_.tensor_add(bvp_sb[:, g:g + 1], PS1[:, 770:771], bv_sb[:, g:g + 1]))
            v_bv = vch.v
            prog.wait("scalar", vch, v_bq)
            t_q, t_k = [], []
            pend_norm = [None]
            norm_mm_vals = {}
            v_qred = [0]

            def flush_norm():
                if pend_norm[0] is None:
                    return
                sv, j, buf = pend_norm[0]
                pend_norm[0] = None
                prog.wait("tensor", sch, sv)
                sl = j % 4
                if j == 4:
                    prog.wait("tensor", vch, v_qred[0])
                fn = lambda E_, sl=sl, buf=buf: E_.matmul(
                    PSB[0:1, 512 * (sl % 2):512 * (sl % 2 + 1)], onescol_bf[:],
                    buf[:], start=(sl < 2), stop=(sl >= 2))
                prog.add("tensor", lambda E_, f=fn: f(E_).then_inc(s_t.h, 1))
                s_t.inc()
                norm_mm_vals[j] = s_t.v

            for tc in range(4):
                ps = (PS0, PSC)[tc % 2]
                prog.wait("tensor", s_ht, 16 * 4 * (tc + 1))
                if tc >= 2:
                    prog.wait("tensor", sch, q_evac_sch[(g, tc - 2)])
                flush_norm()
                for (wdst, cols, tl) in ((wq_bf, (0, 512), t_q), (wk_bf, (512, 1024), t_k)):
                    for c in range(NC_):
                        fn = lambda E_, o=wdst, p=p, c=c, tc=tc, ps=ps, cols=cols: E_.matmul(
                            ps[:, cols[0]:cols[1]], o[:, p, c, :],
                            hf[:, c, 512 * tc:512 * (tc + 1)],
                            start=(c == 0), stop=(c == NC_ - 1))
                        if c == NC_ - 1:
                            prog.add("tensor", lambda E_, f=fn: f(E_).then_inc(s_t.h, 1))
                            s_t.inc()
                        else:
                            prog.add("tensor", fn)
                    tl.append(s_t.v)
                prog.wait("scalar", s_t, t_k[tc])
                sop(lambda E_, p=p, tc=tc, ps=ps: E_.activation(
                    qf[:, p, 512 * tc:512 * (tc + 1)], ps[:, 0:512], AF.Identity,
                    bias=bqp[:, 2 * p:2 * p + 1]))
                sop(lambda E_, p=p, tc=tc, ps=ps: E_.activation(
                    kf[:, p, 512 * tc:512 * (tc + 1)], ps[:, 512:1024], AF.Identity,
                    bias=bqp[:, 2 * p + 1:2 * p + 2]))
                q_evac_sch[(g, tc)] = sch.v
            t_v = []
            vt_base = len(vt_ev)
            for i in range(NT):
                vps = PS1[:, 0:128] if i % 2 == 0 else PS1[:, 512:640]
                prog.wait("tensor", s_ht, 16 * (i + 1))
                if i % 2 == 1 and i == 1:
                    prog.wait("tensor", vch, v_bv)
                if i >= 2:
                    prog.wait("tensor", vch, vt_ev[vt_base + i - 2])
                if i == 6:
                    prog.wait("vector", s_t, norm_mm_vals[3])
                    vdep(lambda E_: E_.tensor_reduce(
                        csr[0:1, 0:1], PSB[0:1, 0:1024], AX.X, ALU.max))
                    v_qred[0] = vch.v
                if 2 <= i <= 9:
                    flush_norm()
                if 1 <= i <= 8:
                    j = i - 1
                    buf = sq_sb if j % 2 == 0 else sq2_sb
                    src = qf if j < 4 else kf
                    if j - 2 in norm_mm_vals:
                        prog.wait("scalar", s_t, norm_mm_vals[j - 2])
                    sdep(lambda E_, src=src, p=p, sl=j % 4, buf=buf: E_.activation(
                        buf[:], src[:, p, 512 * sl:512 * (sl + 1)], AF.Square))
                    pend_norm[0] = (sch.v, j, buf)
                if i == 10:
                    prog.wait("vector", s_t, norm_mm_vals[7])
                    vdep(lambda E_: E_.tensor_reduce(
                        csr[0:1, 1:2], PSB[0:1, 0:1024], AX.X, ALU.max))
                    vdep(lambda E_: E_.tensor_add(csr[0:1, 2:3], csr[0:1, 0:1], csr[0:1, 1:2]))
                    vdep(lambda E_: E_.tensor_scalar_mul(csr[0:1, 3:4], csr[0:1, 2:3], -0.0625))
                    negc_pend[0] = vch.v
                for c in range(NC_):
                    fn = lambda E_, p=p, c=c, i=i, vps=vps: E_.matmul(
                        vps, hf[:, c, 128 * i:128 * (i + 1)], wv_bf[:, p, c, :],
                        start=(c == 0), stop=(c == NC_ - 1))
                    if c == NC_ - 1:
                        prog.add("tensor", lambda E_, f=fn: f(E_).then_inc(s_t.h, 1))
                        s_t.inc()
                    else:
                        prog.add("tensor", fn)
                t_v.append(s_t.v)
                prog.wait("vector", s_t, t_v[i])
                fn = lambda E_, p=p, i=i, vps=vps: E_.tensor_copy(
                    vt5[:, p, i, :, 0:64],
                    vps.rearrange("p (h k) -> p h k", k=64))
                vdep(fn)
                vt_ev.append(vch.v)
            sv_vev_last[g] = vch.v
            wuse_done[g] = s_t.v

            ss_qkev_last[g] = sch.v

            return ss_qkev_last[g]

        def attn_pair(g, s_negc):
            nonlocal exp_sch_prev_pair, vch_prev_pair, sv_ctx_free, st_ctx_free, prev_iter_exp, prev_iter_avlast
            p = g % 2
            prog.wait("tensor", sch, s_negc)
            prog.wait("tensor", vch, sv_vev_last[g])
            prog.wait("tensor", vch, negc_pend[0])
            fn = lambda E_: E_.matmul(PSB[:, 0:1], ones_sb[:], csr[0:1, 3:4],
                                      start=True, stop=True)
            prog.add("tensor", lambda E_, f=fn: f(E_).then_inc(s_t.h, 1))
            s_t.inc()
            prog.wait("scalar", s_t, s_t.v)
            sop(lambda E_: E_.activation(negc_sb[:, 0:1], PSB[:, 0:1], AF.Copy))
            exp_vals = {}
            for h2 in range(2):
                for qc in range(2):
                    q0 = 1024 * qc
                    jmax = 8 * qc + 8
                    t_av_prev2 = {}
                    prog.wait("tensor", sch, prev_iter_exp)
                    all_pieces = {}
                    v_masks = {}

                    def emit_av(j, h2=h2, qc=qc, jmax=jmax):
                        if j == 0:
                            prog.wait("tensor", sch, psc_free[0])
                        prog.wait("tensor", sch, exp_vals[j])
                        if v_masks.get(j):
                            prog.wait("tensor", vch, v_masks[j])
                        for (a, b_, stp) in all_pieces[j]:
                            fn = lambda E_, p=p, h2=h2, j=j, a=a, b_=b_, stp=stp: E_.matmul(
                                PSC[0:65, a:b_], vt5[:, p, j, h2, :],
                                p_sb[:, j % 2, a:b_], start=(j == 0), stop=stp)
                            last = (a, b_, stp) == all_pieces[j][-1]
                            if last:
                                prog.add("tensor", lambda E_, f=fn: f(E_).then_inc(s_t.h, 1))
                                s_t.inc()
                            else:
                                prog.add("tensor", fn)
                        t_av_prev2[j] = s_t.v

                    for j in range(jmax):
                        c0 = max(0, 128 * j - q0)
                        c1 = max(0, 128 * (j + 1) - q0)
                        sps = (PS0, PS1)[j % 2]
                        pieces = []
                        c = c0
                        if j < jmax - 1 and c1 > c0:
                            pieces.append((c0, c1, True))
                            c = c1
                        while c < 1024:
                            ce = min(1024, (c // 512 + 1) * 512)
                            pieces.append((c, ce, j == jmax - 1))
                            c = ce
                        all_pieces[j] = pieces
                        if j >= 2:
                            prog.wait("tensor", sch, exp_vals[j - 2])
                        for (a, b_, _stp) in pieces:
                            fn = lambda E_, p=p, h2=h2, j=j, a=a, b_=b_, sps=sps, q0=q0: E_.matmul(
                                sps[:, a:b_], kf[64 * h2:64 * h2 + 64, p, 128 * j:128 * (j + 1)],
                                qf[64 * h2:64 * h2 + 64, p, q0 + a:q0 + b_],
                                start=True, stop=True)
                            last = (a, b_, _stp) == pieces[-1]
                            if last:
                                prog.add("tensor", lambda E_, f=fn: f(E_).then_inc(s_t.h, 1))
                                s_t.inc()
                            else:
                                prog.add("tensor", fn)
                        t_s = s_t.v
                        prog.wait("scalar", s_t, t_s)
                        if j >= 2:
                            prog.wait("scalar", s_t, t_av_prev2[j - 2])
                        else:
                            prog.wait("scalar", s_t, prev_iter_avlast)
                        sop(lambda E_, j=j, c0=c0, sps=sps: E_.activation(
                            p_sb[:, j % 2, c0:1024], sps[:, c0:1024], AF.Exp,
                            scale=0.125, bias=negc_sb[:, 0:1]))
                        exp_vals[j] = sch.v
                        diag = (128 * j >= q0)
                        if diag:
                            prog.wait("vector", sch, exp_vals[j])
                            vop(lambda E_, j=j, c0=c0: E_.tensor_mul(
                                p_sb[:, j % 2, c0:c0 + 128], p_sb[:, j % 2, c0:c0 + 128], mask_sb[:]))
                            v_masks[j] = vch.v
                        if j >= 1:
                            emit_av(j - 1)
                    emit_av(jmax - 1)
                    t_av_last = s_t.v
                    st_avlast[g] = t_av_last
                    prev_iter_avlast = t_av_last
                    prog.wait("scalar", s_t, t_av_last)
                    sop(lambda E_: E_.activation(rowB[:], PSC[64:65, 0:1024], AF.Ln))
                    sdep(lambda E_: E_.activation(rowB[:], rowB[:], AF.Exp, scale=-1.0))
                    s_r = sch.v
                    prog.wait("scalar", vch, sv_ctx_free)
                    sop(lambda E_: E_.activation(ctx_stage[0:64, :], PSC[0:64, 0:1024], AF.Copy))
                    s_ctxraw = sch.v
                    prog.wait("tensor", sch, s_r)
                    prog.wait("tensor", vch, sv_ctx_free)
                    for half in range(2):
                        fn = lambda E_, half=half: E_.matmul(
                            PSB[0:64, 512 * half:512 * (half + 1)], ones_sb[0:1, 0:64],
                            rowB[0:1, 512 * half:512 * (half + 1)], start=True, stop=True)
                        if half == 1:
                            prog.add("tensor", lambda E_, f=fn: f(E_).then_inc(s_t.h, 1))
                            s_t.inc()
                        else:
                            prog.add("tensor", fn)
                    t_bc = s_t.v
                    prog.wait("vector", sch, s_ctxraw)
                    prog.wait("vector", s_t, t_bc)
                    if h2 == 0:
                        vop(lambda E_, g=g, q0=q0: E_.tensor_mul(
                            ctxv[0:64, g, q0:q0 + 1024], ctx_stage[0:64, :], PSB[0:64, 0:1024]))
                        sv_ctx_free = vch.v
                        st_ctx_free = sch.v
                    else:
                        vop(lambda E_: E_.tensor_mul(
                            ctx_stage[0:64, :], ctx_stage[0:64, :], PSB[0:64, 0:1024]))
                        sv_ctx_free = vch.v
                        v_norm = vch.v
                        prog.wait("gpsimd", s_ht, HT_ALL)
                        prog.wait("gpsimd", vch, v_norm)
                        prog.add("gpsimd", lambda E_, g=g, q0=q0: E_.dma_start(
                            out=ctxv[64:128, g, q0:q0 + 1024], in_=ctx_stage[0:64, :]).then_inc(s_move.h, 16))
                        s_move.inc(16)
                    psc_free[0] = s_ctxraw
                    if h2 == 1:
                        prog.wait("scalar", s_move, s_move.v)
                    prev_iter_exp = sch.v
                    emit_nat(3)
                    if g >= 6:
                        emit_wo_dma(1)
            exp_sch_prev_pair = sch.v
            vch_prev_pair = vch.v

        s_nat = sm("natld")
        nat_jobs = []
        for ic_ in range(NIC):
            nat_jobs.append((wgn_in.rearrange("(a p) h -> p a h", p=128)[:, ic_, :], swg_sb[:, ic_:ic_ + 1], 1024, None))
        for ic_ in range(NIC):
            nat_jobs.append((wun_in.rearrange("(a p) h -> p a h", p=128)[:, ic_, :], swu_sb[:, ic_:ic_ + 1], 1024, None))
        wdr_ = wdn_in.rearrange("(a p) i -> p a i", p=128)
        for hc_ in range(NC_):
            for pc_, (c0_, c1_) in enumerate(((0, 1024), (1024, 2048), (2048, 2816))):
                nat_jobs.append((wdr_[:, hc_, c0_:c1_], umax_acc[:, 89 + pc_:90 + pc_], c1_ - c0_,
                                 hc_ if pc_ == 2 else None))
        nat_state = {"nd": 0, "nr": 0}
        nat_red = {}
        nat_dma = {}

        def emit_nat_dma(k):
            for _ in range(k):
                n = nat_state["nd"]
                if n >= len(nat_jobs) or n - nat_state["nr"] >= 4:
                    return
                src_ap, _, w, _ = nat_jobs[n]
                if n >= 4:
                    prog.wait("sync", vch, nat_red[n - 4])
                prog.add("sync", lambda E_, s=src_ap, w=w, b=n % 4: E_.dma_start(
                    out=natw[:, b, 0:w], in_=s).then_inc(s_nat.h, 16))
                s_nat.inc(16)
                nat_dma[n] = s_nat.v
                nat_state["nd"] = n + 1

        def emit_nat(k):
            for _ in range(k):
                n = nat_state["nr"]
                if n >= len(nat_jobs):
                    return
                if n >= nat_state["nd"]:
                    emit_nat_dma(1)
                _, dst_ap, w, comb = nat_jobs[n]
                prog.wait("vector", s_nat, nat_dma[n])
                vdep(lambda E_, d=dst_ap, w=w, b=n % 4: E_.tensor_reduce(
                    d, natw[:, b, 0:w], AX.X, ALU.max, apply_absolute_value=True))
                nat_red[n] = vch.v
                if comb is not None:
                    vdep(lambda E_: E_.tensor_max(
                        umax_acc[:, 92:93], umax_acc[:, 89:90], umax_acc[:, 90:91]))
                    vdep(lambda E_, hc=comb: E_.tensor_max(
                        swd_sb[:, hc:hc + 1], umax_acc[:, 92:93], umax_acc[:, 91:92]))
                    nat_red[n] = vch.v
                nat_state["nr"] = n + 1
                emit_nat_dma(2)

        prog.wait("tensor", vch, vch.v)
        prog.wait("tensor", s_cld, s_cld.v)
        for c in range(NC_):
            fn = lambda E_, c=c: E_.transpose(
                PS0[0:1, 128 * c:128 * (c + 1)], mod_sb[:, 64 + c:65 + c], id_sb[:])
            if c == NC_ - 1:
                prog.add("tensor", lambda E_, f=fn: f(E_).then_inc(s_t.h, 1))
                s_t.inc()
            else:
                prog.add("tensor", fn)
        prog.wait("scalar", s_t, s_t.v)
        sop(lambda E_: E_.activation(rowB[:], PS0[0:1, 0:1024], AF.Copy))
        s_garow = sch.v
        prog.wait("tensor", sch, s_garow)
        for half in range(2):
            fn = lambda E_, half=half: E_.matmul(
                PSB[:, 512 * half:512 * (half + 1)], ones_sb[:],
                rowB[0:1, 512 * half:512 * (half + 1)], start=True, stop=True)
            if half == 1:
                prog.add("tensor", lambda E_, f=fn: f(E_).then_inc(s_t.h, 1))
                s_t.inc()
            else:
                prog.add("tensor", fn)
        prog.wait("scalar", s_t, s_t.v)
        prog.wait("scalar", vch, vch.v)
        sop(lambda E_: E_.activation(tmp[:, 0:1024], PSB[:, 0:1024], AF.Copy))
        s_gabc = sch.v

        wo_state = {"c": 0, "dma": 0}

        def emit_wo_dma(k):
            for _ in range(k):
                c = wo_state["c"]
                if c >= NC_:
                    return
                if c == 0:
                    assert nat_state["nr"] >= len(nat_jobs), nat_state
                    prog.wait("gpsimd", vch, nat_red[len(nat_jobs) - 1])
                prog.add("gpsimd", lambda E_, c=c: E_.dma_start(
                    out=wo_bf[:, c, :],
                    in_=wo_in.rearrange("(c p) o -> p c o", p=128)[:, c, :]).then_inc(s_wld.h, 16))
                s_wld.inc(16)
                wo_state["dma"] = s_wld.v
                wo_state["c"] = c + 1

        vt_ev = []
        load_pair(0)
        emit_nat_dma(4)
        for g in range(npairs):
            if g + 1 < npairs:
                load_pair(g + 1)
            s_negc = qkv_pair(g)
            if stage >= 3:
                attn_pair(g, s_negc)
                emit_nat(1)

        if stage == 2:
            for nm, buf, shp, w in (("qf", qf, [128, 2 * S], s_t.v),
                                    ("kf", kf, [128, 2 * S], s_t.v),
                                    ("vt", vt, [128, 2 * NT * 130], s_t.v)):
                if nm in dumps:
                    ap = dump(nm, shp, BF16)
                    prog.wait("gpsimd", sch, sch.v)
                    prog.wait("gpsimd", vch, vch.v)
                    prog.add("gpsimd", lambda E_, ap=ap, b=buf: E_.dma_start(out=ap, in_=b[:]).then_inc(s_fin.h, 16))
                    s_fin.inc(16)
            prog.wait("sync", s_fin, s_fin.v)
            prog.wait("sync", sch, sch.v)
            prog.wait("sync", vch, vch.v)
            prog.add("sync", lambda E_: E_.dma_start(
                out=out_hbm.rearrange("(t p) d -> p t d", p=128), in_=x_sb[:]).then_inc(s_fin.h, 16))
            s_fin.inc(16)
            prog.wait("sync", s_fin, s_fin.v)
            prog.wait("sync", s_cld, s_cld.v)
            emit(nc, prog)
            return nc

        emit_wo_dma(NC_)
        prog.wait("vector", vch, vch.v)
        vop(lambda E_: E_.tensor_copy(bvp_bf[:], bvp_sb[:]))
        v_bvbf = vch.v
        prog.wait("tensor", s_wld, wo_state["dma"])
        prog.wait("tensor", vch, v_bvbf)
        prog.wait("tensor", sch, sch.v)
        prog.wait("tensor", vch, vch.v)
        for c in range(NC_):
            for half in range(2):
                fn = lambda E_, c=c, half=half: E_.matmul(
                    PSB[0:1, 512 * half:512 * (half + 1)], bvp_bf[:, c:c + 1],
                    wo_bf[:, c, 512 * half:512 * (half + 1)],
                    start=(c == 0), stop=(c == NC_ - 1))
                if c == NC_ - 1 and half == 1:
                    prog.add("tensor", lambda E_, f=fn: f(E_).then_inc(s_t.h, 1))
                    s_t.inc()
                else:
                    prog.add("tensor", fn)
        t_bo = s_t.v
        prog.wait("vector", s_t, t_bo)
        prog.wait("vector", sch, s_garow)
        prog.wait("vector", s_cld, s_cld.v)
        vdep(lambda E_: E_.tensor_add(rowA[:], PSB[0:1, 0:1024], rowA[:]))
        prog.wait("vector", sch, s_gabc)
        vdep(lambda E_: E_.tensor_mul(rowA[:], rowA[:], tmp[0:1, 0:1024]))
        v_rowA = vch.v
        prog.wait("vector", sch, s_gabc)
        for c in range(NC_):
            vop(lambda E_, c=c: E_.tensor_mul(wo_bf[:, c, :], wo_bf[:, c, :], tmp[:, 0:1024]))
        v_wo = vch.v

        def build_bc_sbuf(dst, mcol):
            prog.wait("tensor", vch, vch.v)
            for c in range(NC_):
                fn = lambda E_, c=c, m=mcol: E_.transpose(
                    PSC[0:1, 128 * c:128 * (c + 1)], mod_sb[:, m + c:m + c + 1], id_sb[:])
                if c == NC_ - 1:
                    prog.add("tensor", lambda E_, f=fn: f(E_).then_inc(s_t.h, 1))
                    s_t.inc()
                else:
                    prog.add("tensor", fn)
            prog.wait("scalar", s_t, s_t.v)
            s_row = sop(lambda E_: E_.activation(rowB[:], PSC[0:1, 0:1024], AF.Copy))
            prog.wait("tensor", sch, s_row)
            for half in range(2):
                fn = lambda E_, half=half: E_.matmul(
                    PSC[:, 512 * half:512 * (half + 1)], ones_sb[:],
                    rowB[0:1, 512 * half:512 * (half + 1)], start=True, stop=True)
                if half == 1:
                    prog.add("tensor", lambda E_, f=fn: f(E_).then_inc(s_t.h, 1))
                    s_t.inc()
                else:
                    prog.add("tensor", fn)
            prog.wait("vector", s_t, s_t.v)
            vop(lambda E_, dst=dst: E_.tensor_copy(dst[:], PSC[:, 0:1024]))
            return vch.v

        def ln2_aggr(a, b):
            w = b - a
            vdep(lambda E_: E_.tensor_tensor(means4[:, a:b], bnall[:, a:b, :, 1], bnall[:, a:b, :, 4], ALU.add))
            vdep(lambda E_: E_.tensor_tensor(m2s4[:, a:b], bnall[:, a:b, :, 2], bnall[:, a:b, :, 5], ALU.add))
            sqv = tmp[:, 832:864].rearrange("p (t u) -> p t u", u=2)
            sqv2 = tmp[:, 864:896].rearrange("p (t u) -> p t u", u=2)
            vdep(lambda E_: E_.tensor_tensor(sqv[:, a:b], bnall[:, a:b, :, 1], bnall[:, a:b, :, 1], ALU.mult))
            vdep(lambda E_: E_.tensor_tensor(sqv2[:, a:b], bnall[:, a:b, :, 4], bnall[:, a:b, :, 4], ALU.mult))
            vdep(lambda E_: E_.tensor_add(sqv[:, a:b], sqv[:, a:b], sqv2[:, a:b]))
            vdep(lambda E_: E_.reduce_sum(tmp[:, 704 + a:704 + b], means4[:, a:b], AX.X))
            vdep(lambda E_: E_.reduce_sum(tmp[:, 720 + a:720 + b], m2s4[:, a:b], AX.X))
            vdep(lambda E_: E_.reduce_sum(tmp[:, 736 + a:736 + b], sqv[:, a:b], AX.X))
            vdep(lambda E_: E_.tensor_scalar_mul(stats[:, 32 + a:32 + b], tmp[:, 704 + a:704 + b], 0.25))
            vdep(lambda E_: E_.tensor_mul(tmp[:, 752 + a:752 + b], stats[:, 32 + a:32 + b], stats[:, 32 + a:32 + b]))
            vdep(lambda E_: E_.tensor_scalar_mul(tmp[:, 832 + a:832 + b], tmp[:, 736 + a:736 + b], 0.25))
            vdep(lambda E_: E_.scalar_tensor_tensor(
                tmp[:, 848 + a:848 + b], tmp[:, 720 + a:720 + b], 1.0 / 1024.0, tmp[:, 832 + a:832 + b], ALU.mult, ALU.add))
            vv = vdep(lambda E_: E_.tensor_sub(tmp[:, a:b], tmp[:, 848 + a:848 + b], tmp[:, 752 + a:752 + b]))
            prog.wait("scalar", vch, vv)
            ss = sdep(lambda E_, a=a, b=b: E_.activation(tmp[:, 16 + a:16 + b], tmp[:, a:b], AF.Sqrt, bias=smalls[:, 4:5]))
            prog.wait("vector", sch, ss)
            vop(lambda E_: E_.reciprocal(stats[:, 48 + a:48 + b], tmp[:, 16 + a:16 + b]))
            vdep(lambda E_: E_.scalar_tensor_tensor(
                umax_acc[:, a:b], stats[:, 32 + a:32 + b], -1.0, stats[:, 48 + a:48 + b], ALU.mult, ALU.mult))
            return vch.v

        agg_v = {}
        maxred_v = {}
        max_jobs = []

        def emit_max_tile(t):
            pb = h2scr[:] if t % 2 == 0 else PSB[:, 0:1024]
            if t >= 2:
                prog.wait("scalar", vch, maxred_v[t - 2])
            else:
                prog.wait("scalar", vch, v_rowA)
            prog.wait("scalar", vch, agg_v[t // 4])
            su = sdep(lambda E_, t=t, pb=pb: E_.activation(
                pb, x_sb[:, t, :], AF.Identity,
                scale=stats[:, 48 + t:49 + t], bias=umax_acc[:, t:t + 1]))
            prog.wait("vector", sch, su)
            vdep(lambda E_, pb=pb: E_.tensor_mul(pb, pb, cb1s_bc[:]))
            vdep(lambda E_, pb=pb: E_.tensor_add(pb, pb, sbs_bc[:]))
            maxred_v[t] = vdep(lambda E_, t=t, pb=pb: E_.tensor_reduce(
                hmax_acc[:, t:t + 1], pb, AX.X, ALU.max, apply_absolute_value=True))

        res_v = {}
        prog.wait("tensor", s_move, s_move.v)
        v_bc0 = None
        for i in range(NT):
            ps = (PS0, PS1)[i % 2]
            if i == 0:
                prog.wait("tensor", vch, v_wo)
                build_bc_sbuf(cb1s_bc, 72)
            if i >= 2:
                prog.wait("tensor", vch, res_v[i - 2])
            for half in range(2):
                fn = lambda E_, half=half, ps=ps: E_.matmul(
                    ps[:, 512 * half:512 * (half + 1)], ones_sb[:],
                    rowA[0:1, 512 * half:512 * (half + 1)], start=True, stop=False)
                prog.add("tensor", fn)
            for c in range(NC_):
                for half in range(2):
                    fn = lambda E_, c=c, half=half, i=i, ps=ps: E_.matmul(
                        ps[:, 512 * half:512 * (half + 1)], ctxv[:, c, 128 * i:128 * (i + 1)],
                        wo_bf[:, c, 512 * half:512 * (half + 1)],
                        start=False, stop=(c == NC_ - 1))
                    if c == NC_ - 1 and half == 1:
                        prog.add("tensor", lambda E_, f=fn: f(E_).then_inc(s_t.h, 1))
                        s_t.inc()
                    else:
                        prog.add("tensor", fn)
            t_o = s_t.v
            t_o_last = t_o
            if i == 0:
                build_bc_sbuf(sbs_bc, 24)
            prog.wait("vector", s_t, t_o)
            vop(lambda E_, i=i, ps=ps: E_.scalar_tensor_tensor(
                x_sb[:, i, :], ps[:, 0:1024], 1.0, x_sb[:, i, :], ALU.mult, ALU.add))
            res_v[i] = vch.v
            vop(lambda E_, i=i: E_.bn_stats(tmp[:, 512 + 16 * i:518 + 16 * i], x_sb[:, i, 0:512]))
            vop(lambda E_, i=i: E_.bn_stats(tmp[:, 520 + 16 * i:526 + 16 * i], x_sb[:, i, 512:1024]))
            if i % 4 == 3:
                g4 = i // 4
                agg_v[g4] = ln2_aggr(4 * g4, 4 * g4 + 4)
                max_jobs.extend(range(4 * g4, 4 * g4 + 4))
            if max_jobs:
                emit_max_tile(max_jobs.pop(0))
        while max_jobs:
            emit_max_tile(max_jobs.pop(0))

        if "x1" in dumps:
            ap = dump("x1", [S, D], F32)
            prog.wait("gpsimd", vch, vch.v)
            prog.add("gpsimd", lambda E_, ap=ap: E_.dma_start(
                out=ap.rearrange("(t p) d -> p t d", p=128), in_=x_sb[:]).then_inc(s_fin.h, 16))
            s_fin.inc(16)
        if "ctx" in dumps:
            ap = dump("ctx", [128, NC_ * S], BF16)
            prog.wait("gpsimd", vch, vch.v)
            prog.wait("gpsimd", s_move, s_move.v)
            prog.add("gpsimd", lambda E_, ap=ap: E_.dma_start(out=ap, in_=htb[:]).then_inc(s_fin.h, 16))
            s_fin.inc(16)

        if stage <= 3:
            prog.wait("sync", vch, vch.v)
            prog.wait("sync", sch, sch.v)
            prog.add("sync", lambda E_: E_.dma_start(
                out=out_hbm.rearrange("(t p) d -> p t d", p=128), in_=x_sb[:]).then_inc(s_fin.h, 16))
            s_fin.inc(16)
            prog.wait("sync", s_fin, s_fin.v)
            prog.wait("sync", s_cld, s_cld.v)
            emit(nc, prog)
            return nc

        gch = sm("gch")
        s_cc = sm("cc")
        s_xqt = sm("xqt")
        s_ffl = sm("ffl")
        s_uio = sm("uio")

        emit_nat(len(nat_jobs))
        g_nat_done = nat_red[len(nat_jobs) - 1]

        v_hmax = vdep(lambda E_: E_.tensor_reduce(hmax_acc[:, 16:17], hmax_acc[:, 0:16], AX.X, ALU.max))
        prog.wait("tensor", vch, v_hmax)
        fn = lambda E_: E_.transpose(PSB[0:1, 768:896], hmax_acc[:, 16:17], id_sb[:])
        prog.add("tensor", lambda E_, f=fn: f(E_).then_inc(s_t.h, 1))
        s_t.inc()
        prog.wait("vector", s_t, s_t.v)
        v_lmax = vdep(lambda E_: E_.tensor_reduce(csr[0:1, 10:11], PSB[0:1, 768:896], AX.X, ALU.max))
        prog.wait("gpsimd", vch, v_lmax)
        prog.add("gpsimd", lambda E_: E_.dma_start(out=cc1_in.ap(), in_=csr[0:1, 8:16]).then_inc(s_cc.h, 16))
        s_cc.inc(16)
        prog.wait("gpsimd", s_cc, s_cc.v)
        prog.add("gpsimd", lambda E_: E_.collective_compute(
            "AllReduce", ALU.max, replica_groups=[list(range(8))],
            ins=[cc1_in.ap()], outs=[cc1_out.ap()]).then_inc(s_cc.h, 1))
        s_cc.inc(1)
        prog.wait("gpsimd", s_cc, s_cc.v)
        prog.add("gpsimd", lambda E_: E_.dma_start(out=csr[0:1, 11:12], in_=cc1_out.ap()[0:1, 2:3]).then_inc(s_cc.h, 16))
        s_cc.inc(16)
        cc1_done = s_cc.v

        prog.wait("vector", vch, g_nat_done)
        vdep(lambda E_: E_.reciprocal(mod_sb[:, 96:104], swd_sb[:]))
        vdep(lambda E_: E_.tensor_scalar_mul(mod_sb[:, 96:104], mod_sb[:, 96:104], 127.0))
        vdep(lambda E_: E_.tensor_mul(mod_sb[:, 104:112], swd_sb[:], mod_sb[:, 80:88]))
        v_gsw = vdep(lambda E_: E_.tensor_scalar_mul(mod_sb[:, 104:112], mod_sb[:, 104:112], 1.0 / 127.0))

        def build_bc_psc(colrange):
            prog.wait("tensor", vch, vch.v)
            for c in range(NC_):
                fn = lambda E_, c=c, m=colrange: E_.transpose(
                    PS1[0:1, 128 * c:128 * (c + 1)], mod_sb[:, m + c:m + c + 1], id_sb[:])
                if c == NC_ - 1:
                    prog.add("tensor", lambda E_, f=fn: f(E_).then_inc(s_t.h, 1))
                    s_t.inc()
                else:
                    prog.add("tensor", fn)
            prog.wait("scalar", s_t, s_t.v)
            s_row = sop(lambda E_: E_.activation(rowB[:], PS1[0:1, 0:1024], AF.Copy))
            prog.wait("tensor", sch, s_row)
            for half in range(2):
                fn = lambda E_, half=half: E_.matmul(
                    PSC[:, 512 * half:512 * (half + 1)], ones_sb[:],
                    rowB[0:1, 512 * half:512 * (half + 1)], start=True, stop=True)
                if half == 1:
                    prog.add("tensor", lambda E_, f=fn: f(E_).then_inc(s_t.h, 1))
                    s_t.inc()
                else:
                    prog.add("tensor", fn)
            return s_t.v

        t_psc = build_bc_psc(96)
        wdv = wdT_in.rearrange("(a p) h -> p a h", p=128)
        prog.wait("vector", s_t, t_o_last)
        D1_HEAD = 6
        wd_q_v = []
        xq_done = [0]

        def emit_d1(icq):
            if icq < D1_HEAD:
                stg = wuT_s[:, icq % 2, :, :].rearrange("p c f -> p (c f)")
            else:
                stg = d1stg[:, icq % 2, :]
            if icq < 2:
                prog.wait("sync", s_t, st_avlast[7])
            if icq >= 2:
                prog.wait("sync", vch, wd_q_v[icq - 2])
            prog.add("sync", lambda E_, stg=stg, icq=icq: E_.dma_start(
                out=stg, in_=wdv[:, icq, :]).then_inc(s_ffl.h, 16))
            s_ffl.inc(16)
            prog.wait("vector", s_ffl, s_ffl.v)
            if icq == 0:
                prog.wait("vector", s_t, t_psc)
            vop(lambda E_, stg=stg: E_.tensor_mul(tmp[:, 0:1024], stg, PSC[:, 0:1024]))
            wd_q_v.append(vdep(lambda E_, icq=icq: E_.tensor_scalar(
                wd_pre[:, icq, :], tmp[:, 0:1024], RND, RND, ALU.add, ALU.subtract)))

        for icq in range(D1_HEAD):
            emit_d1(icq)
        d1_head_v = vch.v

        prog.wait("scalar", s_cc, cc1_done)
        s_sig = sop(lambda E_: E_.activation(csr[0:1, 12:13], csr[0:1, 8:9], AF.Sigmoid))
        prog.wait("vector", sch, s_sig)
        vdep(lambda E_: E_.tensor_mul(csr[0:1, 13:14], csr[0:1, 11:12], csr[0:1, 12:13]))
        vdep(lambda E_: E_.tensor_scalar_mul(csr[0:1, 13:14], csr[0:1, 13:14], 1.0 / 127.0))
        vdep(lambda E_: E_.reciprocal(csr[0:1, 14:15], csr[0:1, 13:14]))
        v_rs = vch.v
        prog.wait("tensor", vch, v_rs)
        fn = lambda E_: E_.matmul(PSB[:, 896:897], ones_sb[:], csr[0:1, 14:15], start=True, stop=True)
        prog.add("tensor", lambda E_, f=fn: f(E_).then_inc(s_t.h, 1))
        s_t.inc()
        fn = lambda E_: E_.matmul(PSB[:, 897:898], ones_sb[:], csr[0:1, 13:14], start=True, stop=True)
        prog.add("tensor", lambda E_, f=fn: f(E_).then_inc(s_t.h, 1))
        s_t.inc()
        t_rs = s_t.v
        prog.wait("vector", s_t, t_rs)
        vdep(lambda E_: E_.tensor_copy(smalls[:, 1:2], PSB[:, 896:897]))
        v_rspp = vdep(lambda E_: E_.tensor_copy(smalls[:, 0:1], PSB[:, 897:898]))
        vdep(lambda E_: E_.tensor_scalar_mul(cb1s_bc[:], cb1s_bc[:], smalls[:, 1:2]))
        v_cbs = vdep(lambda E_: E_.tensor_scalar_mul(sbs_bc[:], sbs_bc[:], smalls[:, 1:2]))
        prog.wait("vector", vch, g_nat_done)
        vdep(lambda E_, : E_.tensor_scalar(sswg_sb[:], swg_sb[:], smalls[:, 0:1], 1.0 / 127.0, ALU.mult, ALU.mult))
        vdep(lambda E_: E_.tensor_scalar(sswu_sb[:], swu_sb[:], smalls[:, 0:1], 1.0 / 127.0, ALU.mult, ALU.mult))
        vdep(lambda E_: E_.reciprocal(swg_sb[:], swg_sb[:]))
        vdep(lambda E_: E_.tensor_scalar_mul(swg_sb[:], swg_sb[:], 127.0))
        vdep(lambda E_: E_.reciprocal(swu_sb[:], swu_sb[:]))
        vdep(lambda E_: E_.tensor_scalar_mul(swu_sb[:], swu_sb[:], 127.0))
        v_rsw = vch.v

        xq_v = []
        xqt_vals = []
        s_u_v = []
        prog.wait("scalar", vch, d1_head_v)
        prog.wait("scalar", vch, v_cbs)
        for i in range(NT):
            pb = tmp[:, 0:1024] if i % 2 == 0 else PS0[:, 0:1024]
            if i >= 2:
                prog.wait("scalar", vch, xq_v[i - 2])
            s_u_v.append(sdep(lambda E_, i=i, pb=pb: E_.activation(
                pb, x_sb[:, i, :], AF.Identity,
                scale=stats[:, 48 + i:49 + i], bias=umax_acc[:, i:i + 1])))
            if i >= 2:
                prog.wait("vector", s_xqt, xqt_vals[i - 2])
            prog.wait("vector", sch, s_u_v[i])
            vdep(lambda E_, pb=pb: E_.tensor_mul(pb, pb, cb1s_bc[:]))
            vdep(lambda E_, pb=pb: E_.tensor_add(pb, pb, sbs_bc[:]))
            vdep(lambda E_, pb=pb: E_.tensor_scalar(pb, pb, RND, RND, ALU.add, ALU.subtract))
            xq_v.append(vdep(lambda E_, i=i, pb=pb: E_.tensor_scalar(
                xqtile[:, i % 2, :], pb, 127.0, -128.0, ALU.min, ALU.max)))
            prog.wait("scalar", vch, xq_v[i])
            prog.add("scalar", lambda E_, i=i: E_.dma_start_transpose(
                out=hf[:, :, 128 * i:128 * (i + 1)],
                in_=xqtile[:, i % 2, :].rearrange("p (c f) -> p c f", c=NC_)).then_inc(s_xqt.h, 16))
            xqt_vals.append(s_xqt.inc(16))
        XQT_ALL = s_xqt.v
        xq_done[0] = vch.v

        if "xq" in dumps:
            ap = dump("xq", [128, NC_ * S], BF16)
            prog.wait("gpsimd", s_xqt, XQT_ALL)
            prog.add("gpsimd", lambda E_, ap=ap: E_.dma_start(out=ap, in_=hfb[:]).then_inc(s_fin.h, 16))
            s_fin.inc(16)
        if "scl" in dumps:
            ap = dump("scl", [128, 16], F32)
            prog.wait("gpsimd", vch, v_cbs)
            prog.add("gpsimd", lambda E_, ap=ap: E_.dma_start(out=ap, in_=smalls[:]).then_inc(s_fin.h, 16))
            s_fin.inc(16)
        if "csr" in dumps:
            ap = dump("csr", [1, 16], F32)
            prog.wait("gpsimd", vch, v_cbs)
            prog.add("gpsimd", lambda E_, ap=ap: E_.dma_start(out=ap, in_=csr[:]).then_inc(s_fin.h, 16))
            s_fin.inc(16)
        if "hacc" in dumps:
            ap = dump("hacc", [128, 32], F32)
            prog.wait("gpsimd", vch, v_cbs)
            prog.add("gpsimd", lambda E_, ap=ap: E_.dma_start(out=ap, in_=hmax_acc[:]).then_inc(s_fin.h, 16))
            s_fin.inc(16)
        if "cbs" in dumps:
            ap = dump("cbs", [128, 2 * D], F32)
            prog.wait("gpsimd", vch, v_cbs)
            prog.add("gpsimd", lambda E_, ap=ap: E_.dma_start(out=ap[:, 0:D], in_=cb1s_bc[:]).then_inc(s_fin.h, 16))
            prog.add("gpsimd", lambda E_, ap=ap: E_.dma_start(out=ap[:, D:2 * D], in_=sbs_bc[:]).then_inc(s_fin.h, 16))
            s_fin.inc(32)

        if stage <= 4:
            prog.wait("sync", vch, vch.v)
            prog.wait("sync", sch, sch.v)
            prog.wait("sync", s_xqt, XQT_ALL)
            prog.add("sync", lambda E_: E_.dma_start(
                out=out_hbm.rearrange("(t p) d -> p t d", p=128), in_=x_sb[:]).then_inc(s_fin.h, 16))
            s_fin.inc(16)
            prog.wait("sync", s_fin, s_fin.v)
            prog.wait("sync", s_cld, s_cld.v)
            emit(nc, prog)
            return nc

        if stage == 5 and "wdp" in dumps:
            ap = dump("wdp", [128, NIC * D], BF16)
            prog.wait("gpsimd", vch, v_wd)
            prog.add("gpsimd", lambda E_, ap=ap: E_.dma_start(
                out=ap, in_=wd_pre[:].rearrange("p a b -> p (a b)")).then_inc(s_fin.h, 16))
            s_fin.inc(16)

        wgv = wgT_in.rearrange("(c p) i -> p c i", p=128)
        wuv = wuT_in.rearrange("(c p) i -> p c i", p=128)
        uo_v = {}
        q_free_v = {}
        t_gu = {}
        ev_g = {}
        u_v = {}
        t_bcq_prev = [0]

        def gate_prep(ic):
            bufp = ic % 2
            r0 = 256 if bufp == 0 else 768
            bc0 = 0 if bufp == 0 else 512
            if ic >= 2:
                prog.wait("sync", vch, q_free_v[ic - 2])
            for (wsl, wv_) in ((wgT_s, wgv), (wuT_s, wuv)):
                prog.add("sync", lambda E_, w=wsl, v=wv_, ic=ic, bufp=bufp: E_.dma_start(
                    out=w[:, bufp, :, :], in_=v[:, :, 128 * ic:128 * (ic + 1)]).then_inc(s_ffl.h, 16))
                s_ffl.inc(16)
            ld_v = s_ffl.v
            prog.wait("tensor", vch, v_rsw)
            if ic >= 2:
                prog.wait("tensor", vch, q_free_v[ic - 2])
            for k, scol in enumerate((swg_sb, swu_sb)):
                fn = lambda E_, co=r0 + 128 * k, s=scol, ic=ic: E_.transpose(
                    PSB[0:1, co:co + 128], s[:, ic:ic + 1], id_sb[:])
                prog.add("tensor", lambda E_, f=fn: f(E_).then_inc(s_t.h, 1))
                s_t.inc()
            t_trs = s_t.v
            prog.wait("scalar", s_t, t_trs)
            prog.wait("scalar", s_t, t_bcq_prev[0])
            sop(lambda E_, r0=r0: E_.activation(rowD[0:1, 0:128], PSB[0:1, r0:r0 + 128], AF.Copy))
            s_rD = sop(lambda E_, r0=r0: E_.activation(rowB[0:1, 0:128], PSB[0:1, r0 + 128:r0 + 256], AF.Copy))
            prog.wait("tensor", sch, s_rD)
            for k, rsrc in enumerate((rowD[0:1, 0:128], rowB[0:1, 0:128])):
                fn = lambda E_, co=bc0 + 128 * k, r=rsrc: E_.matmul(
                    PSB[:, co:co + 128], ones_sb[:], r, start=True, stop=True)
                prog.add("tensor", lambda E_, f=fn: f(E_).then_inc(s_t.h, 1))
                s_t.inc()
            t_bcq_prev[0] = s_t.v
            prog.wait("vector", s_ffl, ld_v)
            prog.wait("vector", s_t, t_bcq_prev[0])
            if (ic - 2, 3) in t_gu:
                prog.wait("vector", s_t, t_gu[(ic - 2, 3)])
            for (wsl, qd, co) in ((wgT_s, wgq, bc0), (wuT_s, wuq, bc0 + 128)):
                for hc in range(NC_):
                    vop(lambda E_, w=wsl, bufp=bufp, hc=hc, co=co: E_.tensor_mul(
                        tmp[:, 0:128], w[:, bufp, hc, :], PSB[:, co:co + 128]))
                    vdep(lambda E_, q=qd, bufp=bufp, hc=hc: E_.tensor_scalar(
                        q[:, bufp, hc, :], tmp[:, 0:128], RND, RND, ALU.add, ALU.subtract))
            q_free_v[ic] = vch.v

        def gate_mms(ic):
            bufp = ic % 2
            prog.wait("tensor", vch, q_free_v[ic])
            prog.wait("tensor", s_xqt, XQT_ALL)
            for tc in range(4):
                ps = (PS0, PS1)[tc % 2]
                prev = (ic, tc - 2) if tc >= 2 else (ic - 1, tc + 2)
                if prev in ev_g:
                    prog.wait("tensor", sch, ev_g[prev])
                for (qd, cols) in ((wgq, (0, 512)), (wuq, (512, 1024))):
                    for c in range(NC_):
                        fn = lambda E_, q=qd, bufp=bufp, c=c, tc=tc, ps=ps, cols=cols: E_.matmul(
                            ps[:, cols[0]:cols[1]], q[:, bufp, c, :],
                            hf[:, c, 512 * tc:512 * (tc + 1)],
                            start=(c == 0), stop=(c == NC_ - 1))
                        if c == NC_ - 1:
                            prog.add("tensor", lambda E_, f=fn: f(E_).then_inc(s_t.h, 1))
                            s_t.inc()
                        else:
                            prog.add("tensor", fn)
                t_gu[(ic, tc)] = s_t.v
                prog.wait("scalar", s_t, t_gu[(ic, tc)])
                prevsg = (ic, tc - 2) if tc >= 2 else (ic - 1, tc + 2)
                if prevsg in u_v:
                    prog.wait("scalar", vch, u_v[prevsg])
                sop(lambda E_, ic=ic, tc=tc, ps=ps: E_.activation(
                    sg_sb[:, tc % 2, :], ps[:, 0:512], AF.Silu, scale=sswg_sb[:, ic:ic + 1]))
                ev = sop(lambda E_, ic=ic, tc=tc, ps=ps: E_.activation(
                    up_sb[:, tc % 2, :], ps[:, 512:1024], AF.Identity, scale=sswu_sb[:, ic:ic + 1]))
                ev_g[(ic, tc)] = ev
                prog.wait("vector", sch, ev)
                prevu = (ic, tc - 2) if tc >= 2 else (ic - 1, tc + 2)
                if prevu in uo_v:
                    prog.wait("vector", s_uio, uo_v[prevu])
                vop(lambda E_, tc=tc: E_.tensor_mul(u_sb[:, tc % 2, :], sg_sb[:, tc % 2, :], up_sb[:, tc % 2, :]))
                u_v[(ic, tc)] = vdep(lambda E_, ic=ic, tc=tc: E_.tensor_reduce(
                    umax_acc[:, 4 * ic + tc:4 * ic + tc + 1], u_sb[:, tc % 2, :],
                    AX.X, ALU.max, apply_absolute_value=True))
                prog.wait("sync", vch, u_v[(ic, tc)])
                prog.add("sync", lambda E_, ic=ic, tc=tc: E_.dma_start(
                    out=u_scr[ic, :, 512 * tc:512 * (tc + 1)], in_=u_sb[:, tc % 2, :]).then_inc(s_uio.h, 16))
                uo_v[(ic, tc)] = s_uio.inc(16)

        prog.wait("sync", vch, d1_head_v)
        gate_prep(0)
        for ic in range(NIC):
            if ic + 1 < NIC:
                gate_prep(ic + 1)
            gate_mms(ic)
            if D1_HEAD + ic < NIC:
                emit_d1(D1_HEAD + ic)
        U_OUT_ALL = s_uio.v

        if "ffdbg" in dumps:
            ap = dump("ffdbg", [128, 128], F32)
            ap2 = dump("ffq", [128, 4 * NC_ * 128], BF16)
            prog.wait("gpsimd", vch, vch.v)
            prog.wait("gpsimd", sch, sch.v)
            prog.add("gpsimd", lambda E_, ap=ap: E_.dma_start(out=ap[:, 0:22], in_=sswg_sb[:]).then_inc(s_fin.h, 16))
            prog.add("gpsimd", lambda E_, ap=ap: E_.dma_start(out=ap[:, 22:44], in_=sswu_sb[:]).then_inc(s_fin.h, 16))
            prog.add("gpsimd", lambda E_, ap=ap: E_.dma_start(out=ap[:, 44:66], in_=swg_sb[:]).then_inc(s_fin.h, 16))
            prog.add("gpsimd", lambda E_, ap=ap: E_.dma_start(out=ap[:, 66:67], in_=smalls[:, 0:1], allow_slow_non_contiguous=True).then_inc(s_fin.h, 16))
            s_fin.inc(64)
            prog.add("gpsimd", lambda E_, ap=ap2: E_.dma_start(
                out=ap[:, 0:NC_ * 128], in_=wgq[:, 0, :, :].rearrange("p c f -> p (c f)")).then_inc(s_fin.h, 16))
            prog.add("gpsimd", lambda E_, ap=ap2: E_.dma_start(
                out=ap[:, NC_ * 128:2 * NC_ * 128], in_=wgq[:, 1, :, :].rearrange("p c f -> p (c f)")).then_inc(s_fin.h, 16))
            prog.add("gpsimd", lambda E_, ap=ap2: E_.dma_start(
                out=ap[:, 2 * NC_ * 128:3 * NC_ * 128], in_=wuq[:, 0, :, :].rearrange("p c f -> p (c f)")).then_inc(s_fin.h, 16))
            prog.add("gpsimd", lambda E_, ap=ap2: E_.dma_start(
                out=ap[:, 3 * NC_ * 128:4 * NC_ * 128], in_=wuq[:, 1, :, :].rearrange("p c f -> p (c f)")).then_inc(s_fin.h, 16))
            s_fin.inc(64)

        v_umax = vdep(lambda E_: E_.tensor_reduce(umax_acc[:, 88:89], umax_acc[:, 0:88], AX.X, ALU.max))
        prog.wait("tensor", vch, v_umax)
        fn = lambda E_: E_.transpose(PSB[0:1, 768:896], umax_acc[:, 88:89], id_sb[:])
        prog.add("tensor", lambda E_, f=fn: f(E_).then_inc(s_t.h, 1))
        s_t.inc()
        prog.wait("vector", s_t, s_t.v)
        v_lmax2 = vdep(lambda E_: E_.tensor_reduce(csr[0:1, 10:11], PSB[0:1, 768:896], AX.X, ALU.max))
        prog.wait("gpsimd", vch, v_lmax2)
        prog.add("gpsimd", lambda E_: E_.dma_start(out=cc2_in.ap(), in_=csr[0:1, 8:16]).then_inc(s_cc.h, 16))
        s_cc.inc(16)
        prog.wait("gpsimd", s_cc, s_cc.v)
        prog.add("gpsimd", lambda E_: E_.collective_compute(
            "AllReduce", ALU.max, replica_groups=[list(range(8))],
            ins=[cc2_in.ap()], outs=[cc2_out.ap()]).then_inc(s_cc.h, 1))
        s_cc.inc(1)
        prog.wait("gpsimd", s_cc, s_cc.v)
        prog.add("gpsimd", lambda E_: E_.dma_start(out=csr[0:1, 11:12], in_=cc2_out.ap()[0:1, 2:3]).then_inc(s_cc.h, 16))
        s_cc.inc(16)
        cc2_done = s_cc.v

        prog.wait("tensor", sch, sch.v)
        t_psc2 = build_bc_psc(104)
        for icq in range(NIC):
            if icq == 0:
                prog.wait("vector", s_t, t_psc2)
            vop(lambda E_, icq=icq: E_.tensor_mul(wd_pre[:, icq, :], wd_pre[:, icq, :], PSC[:, 0:1024]))
        v_wd = vch.v

        prog.wait("scalar", s_cc, cc2_done)
        s_sig2 = sop(lambda E_: E_.activation(csr[0:1, 12:13], csr[0:1, 9:10], AF.Sigmoid))
        prog.wait("vector", sch, s_sig2)
        vdep(lambda E_: E_.tensor_mul(csr[0:1, 13:14], csr[0:1, 11:12], csr[0:1, 12:13]))
        vdep(lambda E_: E_.tensor_scalar_mul(csr[0:1, 13:14], csr[0:1, 13:14], 1.0 / 127.0))
        vdep(lambda E_: E_.reciprocal(csr[0:1, 14:15], csr[0:1, 13:14]))
        v_rs2 = vch.v
        prog.wait("tensor", vch, v_rs2)
        fn = lambda E_: E_.matmul(PSB[:, 896:897], ones_sb[:], csr[0:1, 14:15], start=True, stop=True)
        prog.add("tensor", lambda E_, f=fn: f(E_).then_inc(s_t.h, 1))
        s_t.inc()
        fn = lambda E_: E_.matmul(PSB[:, 897:898], ones_sb[:], csr[0:1, 13:14], start=True, stop=True)
        prog.add("tensor", lambda E_, f=fn: f(E_).then_inc(s_t.h, 1))
        s_t.inc()
        t_rs2 = s_t.v
        prog.wait("vector", s_t, t_rs2)
        vdep(lambda E_: E_.tensor_copy(smalls[:, 3:4], PSB[:, 896:897]))
        v_s2pp = vdep(lambda E_: E_.tensor_copy(smalls[:, 2:3], PSB[:, 897:898]))

        if "s2row" in dumps:
            ap = dump("s2row", [1, 32], F32)
            prog.wait("gpsimd", vch, v_s2pp)
            prog.add("gpsimd", lambda E_, ap=ap: E_.dma_start(out=ap[0:1, 0:16], in_=csr[:]).then_inc(s_fin.h, 16))
            prog.add("gpsimd", lambda E_, ap=ap: E_.dma_start(out=ap[0:1, 16:32], in_=smalls[0:1, :]).then_inc(s_fin.h, 16))
            s_fin.inc(32)
        if "u0" in dumps:
            ap = dump("u0", [128, S], F32)
            prog.wait("gpsimd", s_uio, U_OUT_ALL)
            prog.add("gpsimd", lambda E_, ap=ap: E_.dma_start(out=ap, in_=u_scr[0]).then_inc(s_fin.h, 16))
            s_fin.inc(16)
        if "u21" in dumps:
            ap = dump("u21", [128, S], F32)
            prog.wait("gpsimd", s_uio, U_OUT_ALL)
            prog.add("gpsimd", lambda E_, ap=ap: E_.dma_start(out=ap, in_=u_scr[NIC - 1]).then_inc(s_fin.h, 16))
            s_fin.inc(16)
        if "umid" in dumps:
            ap = dump("umid", [128, 6 * S], F32)
            prog.wait("gpsimd", s_uio, U_OUT_ALL)
            for j, k in enumerate((5, 10, 14, 18, 19, 20)):
                prog.add("gpsimd", lambda E_, ap=ap, j=j, k=k: E_.dma_start(
                    out=ap[:, S * j:S * (j + 1)], in_=u_scr[k]).then_inc(s_fin.h, 16))
                s_fin.inc(16)

        dn_res = {}
        dn_t = {}
        qd_v = {}
        uv2 = u_scr.rearrange("a p t -> p a t")
        for i in range(NT):
            ps = (PS0, PS1)[i % 2]
            bufp = i % 2
            if i == 0:
                prog.wait("sync", s_uio, U_OUT_ALL)
            if i >= 2:
                prog.wait("sync", s_t, dn_t[i - 2])
            prog.add("sync", lambda E_, i=i, bufp=bufp: E_.dma_start(
                out=uin[:, bufp, :, :], in_=uv2[:, :, 128 * i:128 * (i + 1)]).then_inc(s_uio.h, 16))
            u_in = s_uio.inc(16)
            prog.wait("vector", s_uio, u_in)
            if i == 0:
                prog.wait("vector", vch, v_s2pp)
                prog.wait("vector", vch, v_wd)
            uflat = uin[:, bufp, :, :].rearrange("p a b -> p (a b)")
            hflat = h2q_s2[:, bufp, :, :].rearrange("p a b -> p (a b)")
            vdep(lambda E_, u=uflat: E_.tensor_scalar(u, u, smalls[:, 3:4], RND, ALU.mult, ALU.add))
            vdep(lambda E_, u=uflat: E_.tensor_scalar(u, u, RND, 127.0, ALU.subtract, ALU.min))
            qd_v[i] = vdep(lambda E_, u=uflat, h=hflat: E_.tensor_scalar(h, u, -128.0, None, ALU.max))
            prog.wait("tensor", vch, qd_v[i])
            if i >= 2:
                prog.wait("tensor", vch, dn_res[i - 2])
            for ic in range(NIC):
                for half in range(2):
                    fn = lambda E_, i=i, ic=ic, half=half, bufp=bufp, ps=ps: E_.matmul(
                        ps[:, 512 * half:512 * (half + 1)], h2q_s2[:, bufp, ic, :],
                        wd_pre[:, ic, 512 * half:512 * (half + 1)],
                        start=(ic == 0), stop=(ic == NIC - 1))
                    if ic == NIC - 1 and half == 1:
                        prog.add("tensor", lambda E_, f=fn: f(E_).then_inc(s_t.h, 1))
                        s_t.inc()
                    else:
                        prog.add("tensor", fn)
            dn_t[i] = s_t.v
            prog.wait("vector", s_t, dn_t[i])
            dn_res[i] = vop(lambda E_, i=i, ps=ps: E_.scalar_tensor_tensor(
                x_sb[:, i, :], ps[:, 0:1024], smalls[:, 2:3], x_sb[:, i, :], ALU.mult, ALU.add))

        prog.wait("sync", vch, dn_res[NT - 1])
        prog.add("sync", lambda E_: E_.dma_start(
            out=out_hbm.rearrange("(t p) d -> p t d", p=128), in_=x_sb[:]).then_inc(s_fin.h, 16))
        s_fin.inc(16)
        prog.wait("sync", s_fin, s_fin.v)
        prog.wait("sync", s_cld, s_cld.v)
        prog.wait("sync", s_move, s_move.v)
        prog.wait("sync", s_uio, s_uio.v)
        emit(nc, prog)
        return nc

    return nc


def emit(nc, prog):
    with nc.Block() as blk:
        @blk.sync
        def _(e):
            for fn in prog.ops["sync"]:
                fn(e)

        @blk.gpsimd
        def _(e):
            for fn in prog.ops["gpsimd"]:
                fn(e)

        @blk.tensor
        def _(e):
            for fn in prog.ops["tensor"]:
                fn(e)

        @blk.vector
        def _(e):
            for fn in prog.ops["vector"]:
                fn(e)

        @blk.scalar
        def _(e):
            for fn in prog.ops["scalar"]:
                fn(e)


def make_inputs(inputs):
    x = np.ascontiguousarray(inputs["L0_attn_embeddings"])
    c = np.ascontiguousarray(inputs["L0_c"])
    shared = {
        "wq": np.ascontiguousarray(inputs["L0_attn_Wq"]),
        "wk": np.ascontiguousarray(inputs["L0_attn_Wk"]),
        "wv": np.ascontiguousarray(inputs["L0_attn_Wv"]),
        "bq_f": np.ascontiguousarray(np.asarray(inputs["L0_attn_bq"]).reshape(8, 128).T),
        "bk_f": np.ascontiguousarray(np.asarray(inputs["L0_attn_bk"]).reshape(8, 128).T),
        "bv_f": np.ascontiguousarray(np.asarray(inputs["L0_attn_bv"]).reshape(8, 128).T),
        "bo_row": np.ascontiguousarray(np.asarray(inputs["L0_attn_bo"]).reshape(1, 1024)),
        "wo": np.ascontiguousarray(inputs["L0_attn_Wo"]),
        "wmod": np.ascontiguousarray(inputs["L0_wmod"]),
        "clipx": np.ascontiguousarray(np.asarray(inputs["L0_ffn_clip_x"]).reshape(1, 1)),
        "clipd": np.ascontiguousarray(np.asarray(inputs["L0_ffn_clip_down"]).reshape(1, 1)),
        "wg_nat": np.ascontiguousarray(inputs["L0_ffn_w_gate"]),
        "wu_nat": np.ascontiguousarray(inputs["L0_ffn_w_up"]),
        "wd_nat": np.ascontiguousarray(inputs["L0_ffn_w_down"]),
        "wgT": np.ascontiguousarray(np.asarray(inputs["L0_ffn_w_gate"]).T),
        "wuT": np.ascontiguousarray(np.asarray(inputs["L0_ffn_w_up"]).T),
        "wdT": np.ascontiguousarray(np.asarray(inputs["L0_ffn_w_down"]).T),
    }
    per_core = []
    for b in range(B):
        m = dict(shared)
        m["x"] = np.ascontiguousarray(x[b])
        m["c_f"] = np.ascontiguousarray(c[b].reshape(8, 128).T)
        per_core.append(m)
    return per_core


def kernel(**inputs):
    in_maps = make_inputs(inputs)
    nc = build_nc()
    res = run_bass_kernel_spmd(nc, in_maps, core_ids=list(range(B)))
    return np.stack([r["out"] for r in res.results], axis=0)
